# version 02 (final)
# baseline (speedup 1.0000x reference)
import math
from contextlib import ExitStack

import numpy as np
import ml_dtypes

import concourse.bass as bass
import concourse.mybir as mybir
from concourse.bass_utils import run_bass_kernel_spmd
from concourse.mybir import EngineType

F32 = mybir.dt.float32
BF16 = mybir.dt.bfloat16
U32 = mybir.dt.uint32
AF = mybir.ActivationFunctionType
ALU = mybir.AluOpType
AX = mybir.AxisListType
BF = ml_dtypes.bfloat16

ENGINES = ("sync", "tensor", "vector", "scalar", "gpsimd")
D = 1024
NCH = 8
EPS = 1e-5


class Prog:

    def __init__(self, nc):
        self.nc = nc
        self.recs = {e: [] for e in ENGINES}
        self.cnt = {}
        self.sems = {}

    def inc(self, name, by=1):
        self.cnt[name] = self.cnt.get(name, 0) + by
        return self.cnt[name]

    def cur(self, name):
        return self.cnt.get(name, 0)

    def touch(self, name):
        self.cnt.setdefault(name, 0)

    def on(self, eng, builder, waits=(), inc=None, inc_by=1):
        tgt = self.inc(inc, inc_by) if inc else None
        waits = tuple(waits)
        for s, _ in waits:
            self.touch(s)
        fence = bool(inc) and inc_by == 1 and eng in ("vector", "scalar", "gpsimd")

        def f(engobj, _b=builder, _w=waits, _inc=inc, _by=inc_by, _fence=fence):
            for s, v in _w:
                engobj.wait_ge(self.sems[s], v)
            instr = _b(engobj)
            if _inc:
                assert instr is not None
                if _fence:
                    engobj.drain()
                    engobj.nop().then_inc(self.sems[_inc], _by)
                else:
                    instr.then_inc(self.sems[_inc], _by)
        self.recs[eng].append(f)
        return tgt

    def raw(self, eng, f):
        self.recs[eng].append(f)

    def finalize_into_block(self, block):
        for e in ENGINES:
            if self.recs[e]:
                getattr(block, e)(lambda eng, L=tuple(self.recs[e]): [f(eng) for f in L])


def build_and_run(build_fn, in_maps, core_ids, **kw):
    nc = build_fn()
    return run_bass_kernel_spmd(nc, in_maps, core_ids=core_ids, **kw)


def finish(nc, prog, es):
    for name in list(prog.cnt):
        prog.sems[name] = es.enter_context(nc.semaphore(name))
    with nc.Block() as block:
        prog.finalize_into_block(block)
    es.close()
    return nc


class LNScratch:
    def __init__(self, nc, es, tag=""):
        t = lambda n, shp, dt: es.enter_context(nc.sbuf_tensor(f"lns_{tag}_{n}", shp, dt))
        self.sq = t("sq", [128, D], F32)
        self.s1 = t("s1", [128, 1], F32)
        self.s2 = t("s2", [128, 1], F32)
        self.m = t("m", [128, 1], F32)
        self.msq = t("msq", [128, 1], F32)
        self.var = t("var", [128, 1], F32)
        self.sr = t("sr", [128, 1], F32)
        self.rstd = t("rstd", [128, 1], F32)


def emit_ln(prog, lns, xt, out_ap, ready, done):
    p = prog
    sc = done + "_sc"
    prev_done = p.cur(done)
    w0 = list(ready) + ([(done, prev_done)] if prev_done else [])
    t1 = p.on("scalar", lambda e: e.activation(lns.sq[:], xt, AF.Square),
              waits=w0, inc=sc)

    def dve1(v):
        v.reduce_sum(lns.s1[:], xt, AX.X)
        v.reduce_sum(lns.s2[:], lns.sq[:], AX.X)
        v.drain()
        v.tensor_scalar(lns.m[:], lns.s1[:], 1.0 / D, None, ALU.mult)
        v.tensor_scalar(lns.var[:], lns.s2[:], 1.0 / D, None, ALU.mult)
        v.drain()
        v.tensor_scalar(lns.msq[:], lns.m[:], lns.m[:, 0:1], None, ALU.mult)
        v.drain()
        v.tensor_sub(lns.var[:], lns.var[:], lns.msq[:])
        v.drain()
        return v.tensor_scalar(lns.var[:], lns.var[:], EPS, None, ALU.add)
    t2 = p.on("vector", dve1, waits=list(ready) + [(sc, t1)], inc=sc)
    t3 = p.on("scalar", lambda e: e.activation(lns.sr[:], lns.var[:], AF.Sqrt),
              waits=[(sc, t2)], inc=sc)

    def dve2(v):
        v.reciprocal(lns.rstd[:], lns.sr[:])
        v.drain()
        return v.tensor_scalar(out_ap, xt, lns.m[:], lns.rstd[:],
                               ALU.subtract, ALU.mult)
    p.on("vector", dve2, waits=[(sc, t3)], inc=done)


def build_ln0(T):
    nc = bass.Bass("TRN2", target_bir_lowering=False, debug=False)
    xin = nc.dram_tensor("xin", [T, D], F32, kind="ExternalInput").ap()
    lnx = nc.dram_tensor("lnx", [T, D], BF16, kind="ExternalOutput").ap()
    es = ExitStack()
    xt = [es.enter_context(nc.sbuf_tensor(f"xt{i}", [128, D], F32)) for i in range(2)]
    ot = [es.enter_context(nc.sbuf_tensor(f"ot{i}", [128, D], BF16)) for i in range(2)]
    lns = LNScratch(nc, es)
    p = Prog(nc)
    ntile = T // 128
    for i in range(ntile):
        b = i % 2
        waits = []
        if i >= 2:
            waits.append(("ln_done", i - 1))
        t_ld = p.on("sync", lambda e, i=i, b=b: e.dma_start(
            out=xt[b][:], in_=xin[i * 128:(i + 1) * 128, :]),
            waits=waits, inc="ld", inc_by=16)
        ready = [("ld", t_ld)]
        if i >= 2:
            ready.append(("st", 16 * (i - 1)))
        emit_ln(p, lns, xt[b][:], ot[b][:], ready, "ln_done")
        t_done = p.cur("ln_done")
        p.on("sync", lambda e, i=i, b=b: e.dma_start(
            out=lnx[i * 128:(i + 1) * 128, :], in_=ot[b][:]),
            waits=[("ln_done", t_done)], inc="st", inc_by=16)
    p.on("sync", lambda e: e.nop(), waits=[("st", p.cur("st"))])
    return finish(nc, p, es)


def alloc_psum(nc, es, n=8):
    return [es.enter_context(nc.psum_tensor(f"ps{i}", [128, 512], F32))
            for i in range(n)]


def emit_identity(p, ident_ap, n=128):
    def f(g):
        g.memset(ident_ap, 1.0)
        g.engine_nop()
        return g.affine_select(ident_ap, ident_ap, [[1, n]], ALU.is_equal, 0.0,
                               base=0, channel_multiplier=-1)
    return p.on("gpsimd", f, inc="ident")


def emit_scale_cols(p, nc, es, name, vec_dram, nch, ident_f32, ident_ready,
                    pst, ld_sem="scl_ld", done_sem="scl"):
    rows = es.enter_context(nc.sbuf_tensor(f"{name}_rows", [nch, 128], F32))
    cols = es.enter_context(nc.sbuf_tensor(f"{name}_cols", [128, nch], F32))
    t = p.on("sync", lambda e: e.dma_start(
        out=rows[:], in_=vec_dram.rearrange("(k p) -> k p", p=128)),
        inc=ld_sem, inc_by=16)
    prev = p.cur(done_sem)
    tt = p.on("tensor", lambda e: e.matmul(
        pst[0:128, 0:nch], rows[:], ident_f32[0:nch, 0:nch],
        start=True, stop=True, skip_group_check=True),
        waits=[(ld_sem, t), ident_ready, (done_sem, prev)], inc=done_sem)
    p.on("vector", lambda e: e.tensor_copy(cols[:], pst[0:128, 0:nch]),
         waits=[(done_sem, tt)], inc=done_sem)
    return cols


def build_ffn0(T, TT_=512):
    nc = bass.Bass("TRN2", target_bir_lowering=False, debug=False)
    lnx = nc.dram_tensor("lnx", [T, D], BF16, kind="ExternalInput").ap()
    resin = nc.dram_tensor("resin", [T, D], F32, kind="ExternalInput").ap()
    w1T = nc.dram_tensor("w1T", [D, 4096], F32, kind="ExternalInput").ap()
    w2T = nc.dram_tensor("w2T", [D, 4096], F32, kind="ExternalInput").ap()
    h1T = nc.dram_tensor("h1T", [4096, 4096], F32, kind="ExternalInput").ap()
    h2T = nc.dram_tensor("h2T", [4096, 4096], F32, kind="ExternalInput").ap()
    w3T = nc.dram_tensor("w3T", [4096, D], F32, kind="ExternalInput").ap()
    w1s = nc.dram_tensor("w1s", [4096], F32, kind="ExternalInput").ap()
    w2s = nc.dram_tensor("w2s", [4096], F32, kind="ExternalInput").ap()
    h1s = nc.dram_tensor("h1s", [4096], F32, kind="ExternalInput").ap()
    h2s = nc.dram_tensor("h2s", [4096], F32, kind="ExternalInput").ap()
    w3s = nc.dram_tensor("w3s", [D], F32, kind="ExternalInput").ap()
    resout = nc.dram_tensor("resout", [T, D], F32, kind="ExternalOutput").ap()
    lnxout = nc.dram_tensor("lnxout", [T, D], BF16, kind="ExternalOutput").ap()

    es = ExitStack()
    sb = lambda n, shp, dt: es.enter_context(nc.sbuf_tensor(n, shp, dt))
    xfm = sb("xfm", [128, 8 * TT_], BF16)
    hbuf = sb("hbuf", [128, 32 * TT_], BF16)
    g1buf = sb("g1buf", [128, 32 * TT_], BF16)
    w12 = [sb(f"w12_{b}", [128, 2 * 8 * 128], BF16) for b in range(2)]
    hstr = [sb(f"hstr_{b}", [128, 32 * 128], BF16) for b in range(2)]
    w3sb = sb("w3sb", [128, 32 * 1024], BF16)
    w3bc = sb("w3bc", [128, 1024], F32)
    w3row = sb("w3row", [1, 1024], F32)
    ones1 = sb("ones1", [1, 128], F32)
    identf = sb("identf", [128, 128], F32)
    t1sil = sb("t1sil", [128, TT_], F32)
    ysb = sb("ysb", [128, 1024], F32)
    lny = sb("lny", [128, 1024], F32)
    rest = sb("rest", [128, 1024], F32)
    xnew = sb("xnew", [128, 1024], F32)
    lnxt = sb("lnxt", [128, D], BF16)
    ps = alloc_psum(nc, es, 8)
    lns = LNScratch(nc, es)
    p = Prog(nc)

    emit_identity(p, identf[:], 128)
    id_ready = ("ident", p.cur("ident"))
    scl = {}
    for nm, vec in [("w1s", w1s), ("w2s", w2s), ("h1s", h1s), ("h2s", h2s)]:
        scl[nm] = emit_scale_cols(p, nc, es, nm, vec, 32, identf, id_ready, ps[6])
    t_scl = p.cur("scl")
    w3r = w3T.rearrange("(k p) d -> p k d", p=128)
    w3v = w3sb[:].rearrange("p (k d) -> p k d", d=1024)
    for q in range(4):
        p.on("gpsimd", lambda e, q=q: e.dma_start(
            out=w3v[:, q * 8:(q + 1) * 8, :],
            in_=w3r[:, q * 8:(q + 1) * 8, :]), inc="w3ld", inc_by=16)
    p.on("sync", lambda e: e.dma_start(out=w3row[:], in_=w3s.unsqueeze(0)),
         inc="w3ld", inc_by=16)
    p.on("gpsimd", lambda e: e.memset(ones1[:], 1.0), inc="w3ld")
    t_w3 = p.cur("w3ld")
    for q in range(2):
        p.on("tensor", lambda e, q=q: e.matmul(
            ps[6 + q][:], ones1[:], w3row[0:1, q * 512:(q + 1) * 512],
            start=True, stop=True, skip_group_check=True),
            waits=[("w3ld", t_w3), ("scl", t_scl)], inc="w3bc")
    t_bc = p.cur("w3bc")
    p.on("vector", lambda e: (e.tensor_copy(w3bc[:, 0:512], ps[6][:]),
                              e.tensor_copy(w3bc[:, 512:1024], ps[7][:]))[-1],
         waits=[("w3bc", t_bc)], inc="w3bc")
    t_bc2 = p.cur("w3bc")

    ntile = T // TT_
    n128 = TT_ // 128
    for i in range(ntile):
        r0 = i * TT_
        xw = [("h1mm", p.cur("h1mm"))] if i else []
        for c in range(NCH):
            p.on("sync", lambda e, c=c, r0=r0: e.dma_start(
                out=xfm[:, c * TT_:(c + 1) * TT_],
                in_=lnx[r0:r0 + TT_, c * 128:(c + 1) * 128], transpose=True),
                waits=xw, inc="xl", inc_by=16)
        t_xl = p.cur("xl")

        for m in range(32):
            b = m % 2
            wv = w12[b][:].rearrange("p (w k c) -> p w k c", w=2, c=128)
            wl_w = [("h1mm", max(0, p.cur("h1mm") - 1))]
            p.on("gpsimd", lambda e, m=m, wv=wv: e.dma_start(
                out=wv[:, 0],
                in_=w1T.rearrange("(k p) m -> p k m", p=128)[:, :, m * 128:(m + 1) * 128]),
                waits=wl_w, inc="wl", inc_by=16)
            p.on("gpsimd", lambda e, m=m, wv=wv: e.dma_start(
                out=wv[:, 1],
                in_=w2T.rearrange("(k p) m -> p k m", p=128)[:, :, m * 128:(m + 1) * 128]),
                inc="wl", inc_by=16)
            t_wl = p.cur("wl")
            pa, pb = ps[2 * b], ps[2 * b + 1]
            ev_w = [("h1ev", max(0, p.cur("h1ev") - 1))]
            if m < 2:
                ev_w.append(("s3", p.cur("s3")))

            def mm_h1(e, pa=pa, pb=pb, wv=wv):
                r = None
                for k in range(8):
                    e.matmul(pa[:, 0:TT_], wv[:, 0, k], xfm[:, k * TT_:(k + 1) * TT_],
                             start=(k == 0), stop=(k == 7), skip_group_check=True)
                    r = e.matmul(pb[:, 0:TT_], wv[:, 1, k], xfm[:, k * TT_:(k + 1) * TT_],
                                 start=(k == 0), stop=(k == 7), skip_group_check=True)
                return r
            p.on("tensor", mm_h1, waits=[("wl", t_wl), ("xl", t_xl)] + ev_w, inc="h1mm")
            t_mm = p.cur("h1mm")
            p.on("scalar", lambda e, m=m, pa=pa: e.activation(
                t1sil[:], pa[:, 0:TT_], AF.Silu, scale=scl["w1s"][:, m:m + 1]),
                waits=[("h1mm", t_mm), ("scl", t_scl),
                       ("h1ev", p.cur("h1ev"))], inc="h1sil")
            t_sil = p.cur("h1sil")
            p.on("vector", lambda e, m=m, pb=pb: e.scalar_tensor_tensor(
                hbuf[:, m * TT_:(m + 1) * TT_], pb[:, 0:TT_], scl["w2s"][:, m:m + 1],
                t1sil[:], ALU.mult, ALU.mult),
                waits=[("h1sil", t_sil), ("ymm", p.cur("ymm"))], inc="h1ev")
        t_h1 = p.cur("h1ev")

        for phase in range(2):
            wdram = h1T if phase == 0 else h2T
            sname = "h1s" if phase == 0 else "h2s"
            src = hbuf if phase == 0 else g1buf
            dst = g1buf if phase == 0 else hbuf
            mmsem, evsem = f"g{phase}mm", f"g{phase}ev"
            for m in range(32):
                b = m % 2
                wl_w = [(mmsem, max(0, p.cur(mmsem) - 1))]
                if phase == 1 and m < 2:
                    wl_w.append(("g0mm", p.cur("g0mm")))
                p.on("gpsimd", lambda e, m=m, b=b, wdram=wdram: e.dma_start(
                    out=hstr[b][:].rearrange("p (k c) -> p k c", c=128),
                    in_=wdram.rearrange("(k p) m -> p k m", p=128)[:, :, m * 128:(m + 1) * 128]),
                    waits=wl_w, inc="wl", inc_by=16)
                t_wl = p.cur("wl")
                bank = ps[4 + b]
                ev_w = [(evsem, max(0, p.cur(evsem) - 1))]
                if m == 0:
                    if phase == 0:
                        ev_w += [("h1ev", t_h1), ("g1ev", p.cur("g1ev")),
                                 ("g1mm", p.cur("g1mm"))]
                    else:
                        ev_w += [("g0mm", p.cur("g0mm")), ("g0ev", p.cur("g0ev")),
                                 ("ymm", p.cur("ymm"))]

                def mm_g(e, b=b, bank=bank, src=src):
                    r = None
                    for k in range(32):
                        r = e.matmul(bank[:, 0:TT_],
                                     hstr[b][:, k * 128:(k + 1) * 128],
                                     src[:, k * TT_:(k + 1) * TT_],
                                     start=(k == 0), stop=(k == 31),
                                     skip_group_check=True)
                    return r
                p.on("tensor", mm_g, waits=[("wl", t_wl)] + ev_w, inc=mmsem)
                t_mm = p.cur(mmsem)
                p.on("scalar", lambda e, m=m, bank=bank, sname=sname, dst=dst: e.activation(
                    dst[:, m * TT_:(m + 1) * TT_], bank[:, 0:TT_], AF.Gelu,
                    scale=scl[sname][:, m:m + 1]),
                    waits=[(mmsem, t_mm)], inc=evsem)
        t_g2 = p.cur("g1ev")

        for j in range(n128):
            rj = r0 + j * 128
            p.on("sync", lambda e, rj=rj: e.dma_start(
                out=rest[:], in_=resin[rj:rj + 128, :]),
                waits=[("s3", p.cur("s3"))], inc="resld", inc_by=16)
            t_res = p.cur("resld")
            py0, py1 = ps[2 * (j % 2)], ps[2 * (j % 2) + 1]
            ywaits = [("g1ev", t_g2), ("w3ld", t_w3), ("s3", p.cur("s3")),
                      ("h1sil", p.cur("h1sil"))]

            def mm_y(e, j=j, py0=py0, py1=py1):
                r = None
                for k in range(32):
                    lh = hbuf[:, k * TT_ + j * 128: k * TT_ + (j + 1) * 128]
                    e.matmul(py0[:], lh, w3v[:, k, 0:512],
                             start=(k == 0), stop=(k == 31), skip_group_check=True)
                    r = e.matmul(py1[:], lh, w3v[:, k, 512:1024],
                                 start=(k == 0), stop=(k == 31), skip_group_check=True)
                return r
            p.on("tensor", mm_y, waits=ywaits, inc="ymm")
            t_ymm = p.cur("ymm")

            def ev_y(v, py0=py0, py1=py1):
                v.tensor_mul(ysb[:, 0:512], py0[:], w3bc[:, 0:512])
                return v.tensor_mul(ysb[:, 512:1024], py1[:], w3bc[:, 512:1024])
            p.on("vector", ev_y, waits=[("ymm", t_ymm), ("w3bc", t_bc2)], inc="yev")
            t_yev = p.cur("yev")
            emit_ln(p, lns, ysb[:], lny[:], [("yev", t_yev)], "lnyd")
            t_lny = p.cur("lnyd")
            p.on("vector", lambda e: e.tensor_add(xnew[:], lny[:], rest[:]),
                 waits=[("lnyd", t_lny), ("resld", t_res)], inc="xnewd")
            t_xn = p.cur("xnewd")
            p.on("sync", lambda e, rj=rj: e.dma_start(
                out=resout[rj:rj + 128, :], in_=xnew[:]),
                waits=[("xnewd", t_xn)], inc="st", inc_by=16)
            emit_ln(p, lns, xnew[:], lnxt[:], [("xnewd", t_xn)], "lnxd")
            t_lnx = p.cur("lnxd")
            p.on("sync", lambda e, rj=rj: e.dma_start(
                out=lnxout[rj:rj + 128, :], in_=lnxt[:]),
                waits=[("lnxd", t_lnx)], inc="st", inc_by=16)
            p.on("gpsimd", lambda e: e.engine_nop(),
                 waits=[("st", p.cur("st"))], inc="s3")
    p.on("sync", lambda e: e.nop(), waits=[("st", p.cur("st"))])
    return finish(nc, p, es)


def host_prep_ffn0(inp):
    C = np.ascontiguousarray
    return dict(
        w1T=C(inp["L0_ffn_w1_w"].T), w2T=C(inp["L0_ffn_w2_w"].T),
        h1T=C(inp["L0_ffn_h1_w"].T), h2T=C(inp["L0_ffn_h2_w"].T),
        w3T=C(inp["L0_ffn_w3_w"].T),
        w1s=inp["L0_ffn_w1_s"], w2s=inp["L0_ffn_w2_s"],
        h1s=inp["L0_ffn_h1_s"], h2s=inp["L0_ffn_h2_s"],
        w3s=inp["L0_ffn_w3_s"])


class S3Tiles:
    def __init__(self, nc, es, tag=""):
        sb = lambda n, shp, dt: es.enter_context(nc.sbuf_tensor(f"s3{tag}_{n}", shp, dt))
        self.ysb = sb("ysb", [128, D], F32)
        self.lny = self.ysb
        self.rest = sb("rest", [128, D], F32)
        self.xnew = sb("xnew", [128, D], F32)
        self.lnxt = sb("lnxt", [128, D], BF16)


def emit_s3(p, lns, s3t, resin, resout, lnxout, rj, y_ready, final=False):
    p.on("sync", lambda e: e.dma_start(out=s3t.rest[:], in_=resin[rj:rj + 128, :]),
         waits=[("s3", p.cur("s3"))], inc="resld", inc_by=16)
    t_res = p.cur("resld")
    emit_ln(p, lns, s3t.ysb[:], s3t.lny[:], list(y_ready), "lnyd")
    t_lny = p.cur("lnyd")
    p.on("vector", lambda e: e.tensor_add(s3t.xnew[:], s3t.lny[:], s3t.rest[:]),
         waits=[("lnyd", t_lny), ("resld", t_res)], inc="xnewd")
    t_xn = p.cur("xnewd")
    p.on("sync", lambda e: e.dma_start(out=resout[rj:rj + 128, :], in_=s3t.xnew[:]),
         waits=[("xnewd", t_xn)], inc="st", inc_by=16)
    if not final:
        emit_ln(p, lns, s3t.xnew[:], s3t.lnxt[:], [("xnewd", t_xn)], "lnxd")
        t_lnx = p.cur("lnxd")
        p.on("sync", lambda e: e.dma_start(out=lnxout[rj:rj + 128, :], in_=s3t.lnxt[:]),
             waits=[("lnxd", t_lnx)], inc="st", inc_by=16)
    p.on("gpsimd", lambda e: e.engine_nop(), waits=[("st", p.cur("st"))], inc="s3")


ATT0_SCALE = 1.0 / math.sqrt(2048.0)

def build_att0(T):
    GR = 256
    ngroup = T // GR
    nc = bass.Bass("TRN2", target_bir_lowering=False, debug=False)
    lnx = nc.dram_tensor("lnx", [T, D], BF16, kind="ExternalInput").ap()
    resin = nc.dram_tensor("resin", [T, D], F32, kind="ExternalInput").ap()
    Wv = nc.dram_tensor("Wv", [D, 2048], F32, kind="ExternalInput").ap()
    Wg = nc.dram_tensor("Wg", [D, 2048], F32, kind="ExternalInput").ap()
    Win = nc.dram_tensor("Win", [D, 128], F32, kind="ExternalInput").ap()
    Wout = nc.dram_tensor("Wout", [2048, D], F32, kind="ExternalInput").ap()
    bv = nc.dram_tensor("bv", [2048], F32, kind="ExternalInput").ap()
    bg = nc.dram_tensor("bg", [2048], F32, kind="ExternalInput").ap()
    bout = nc.dram_tensor("bout", [D], F32, kind="ExternalInput").ap()
    vecs9 = nc.dram_tensor("vecs9", [9, 128], F32, kind="ExternalInput").ap()
    kv0 = nc.dram_tensor("kv0", [128, 2048], F32, kind="ExternalInput").ap()
    resout = nc.dram_tensor("resout", [T, D], F32, kind="ExternalOutput").ap()
    lnxout = nc.dram_tensor("lnxout", [T, D], BF16, kind="ExternalOutput").ap()
    kvout = nc.dram_tensor("kvout", [128, 2048], F32, kind="ExternalOutput").ap()

    es = ExitStack()
    sb = lambda n, shp, dt: es.enter_context(nc.sbuf_tensor(n, shp, dt))
    wv_sb = sb("wv_sb", [128, 8 * 2048], BF16)
    wg_sb = sb("wg_sb", [128, 8 * 2048], BF16)
    win_sb = sb("win_sb", [128, 8 * 128], BF16)
    wout_sb = sb("wout_sb", [128, 16 * 1024], BF16)
    bv_row = sb("bv_row", [1, 2048], F32)
    bout_row = sb("bout_row", [1, 1024], F32)
    ones1 = sb("ones1", [1, 128], F32)
    identf = sb("identf", [128, 128], F32)
    identb = sb("identb", [128, 128], BF16)
    sc_attn = sb("sc_attn", [128, 1], F32)
    af9 = None
    bgc = None
    xg = sb("xg", [128, 8 * GR], BF16)
    vg = [sb(f"vg{t}", [128, 2048], BF16) for t in range(2)]
    gateg = sb("gateg", [128, 16 * GR], BF16)
    xq = sb("xq", [128, GR], F32)
    qq = sb("qq", [128, GR], BF16)
    qk = sb("qk", [128, GR], BF16)
    lq = sb("lq", [128, GR], BF16)
    lkpre = sb("lkpre", [128, GR], BF16)
    lkT = [sb(f"lkT{t}", [128, 128], BF16) for t in range(2)]
    ta = sb("ta", [128, GR + 128], F32)
    asq = sb("asq", [128, GR + 128], BF16)
    og = sb("og", [128, 16 * GR], BF16)
    kvf = sb("kvf", [128, 2048], F32)
    kvb = sb("kvb", [128, 2048], BF16)
    ps = alloc_psum(nc, es, 8)
    lns = LNScratch(nc, es)
    s3t = S3Tiles(nc, es)
    p = Prog(nc)

    emit_identity(p, identf[:], 128)
    p.on("gpsimd", lambda e: (e.memset(identb[:], 1.0), e.engine_nop(),
                              e.affine_select(identb[:], identb[:], [[1, 128]],
                                              ALU.is_equal, 0.0, base=0,
                                              channel_multiplier=-1))[-1], inc="ident")
    p.on("gpsimd", lambda e: e.memset(sc_attn[:], ATT0_SCALE), inc="ident")
    p.on("gpsimd", lambda e: e.memset(ones1[:], 1.0), inc="ident")
    id_ready = ("ident", p.cur("ident"))
    af9 = emit_scale_cols(p, nc, es, "af9", vecs9.flatten(), 9, identf, id_ready, ps[6])
    bgc = emit_scale_cols(p, nc, es, "bgc", bg, 16, identf, id_ready, ps[6])
    t_scl = p.cur("scl")
    for nm, dram, dst, kch in [("wv", Wv, wv_sb, 8), ("wg", Wg, wg_sb, 8),
                               ("win", Win, win_sb, 8)]:
        p.on("gpsimd", lambda e, dram=dram, dst=dst, kch=kch: e.dma_start(
            out=dst[:].rearrange("p (k e) -> p k e", k=kch),
            in_=dram.rearrange("(k p) e -> p k e", p=128)), inc="wld", inc_by=16)
    for q in range(2):
        p.on("gpsimd", lambda e, q=q: e.dma_start(
            out=wout_sb[:].rearrange("p (k e) -> p k e", k=16)[:, q * 8:(q + 1) * 8, :],
            in_=Wout.rearrange("(k p) e -> p k e", p=128)[:, q * 8:(q + 1) * 8, :]),
            inc="wld", inc_by=16)
    p.on("sync", lambda e: e.dma_start(out=bv_row[:], in_=bv.unsqueeze(0)),
         inc="wld", inc_by=16)
    p.on("sync", lambda e: e.dma_start(out=bout_row[:], in_=bout.unsqueeze(0)),
         inc="wld", inc_by=16)
    p.on("sync", lambda e: e.dma_start(out=kvf[:], in_=kv0), inc="wld", inc_by=16)
    t_wld = p.cur("wld")
    p.on("vector", lambda e: e.tensor_copy(kvb[:], kvf[:]),
         waits=[("wld", t_wld)], inc="kvup")

    wvv = wv_sb[:].rearrange("p (k e) -> p k e", k=8)
    wgv = wg_sb[:].rearrange("p (k e) -> p k e", k=8)
    wov = wout_sb[:].rearrange("p (k e) -> p k e", k=16)

    for g in range(ngroup):
        G0 = g * GR
        xw = [("xcons", p.cur("xcons"))] if g else []
        for c in range(NCH):
            p.on("sync", lambda e, c=c, G0=G0: e.dma_start(
                out=xg[:, c * GR:(c + 1) * GR],
                in_=lnx[G0:G0 + GR, c * 128:(c + 1) * 128], transpose=True),
                waits=xw, inc="xl", inc_by=16)
        t_xl = p.cur("xl")

        for tj in range(2):
            for et in range(4):
                bank = ps[2 * tj + (et % 2)]
                ev_w = [("vev", max(0, p.cur("vev") - 1))]

                def mm_v(e, tj=tj, et=et, bank=bank):
                    for k in range(8):
                        e.matmul(bank[:], xg[:, k * GR + tj * 128:k * GR + (tj + 1) * 128],
                                 wvv[:, k, et * 512:(et + 1) * 512],
                                 start=(k == 0), stop=False, skip_group_check=True)
                    return e.matmul(bank[:], ones1[:], bv_row[0:1, et * 512:(et + 1) * 512],
                                    start=False, stop=True, skip_group_check=True)
                p.on("tensor", mm_v, waits=[("wld", t_wld), ("xl", t_xl),
                                            ("kvup", p.cur("kvup"))] + ev_w, inc="vmm")
                t_mm = p.cur("vmm")
                p.on("scalar", lambda e, tj=tj, et=et, bank=bank: e.activation(
                    vg[tj][:, et * 512:(et + 1) * 512], bank[:], AF.Silu),
                    waits=[("vmm", t_mm)], inc="vev")
        t_vev = p.cur("vev")

        for m in range(16):
            bank = ps[4 + (m % 2)]
            ev_w = [("gev", max(0, p.cur("gev") - 1))]
            if m < 2:
                ev_w += [("voev", p.cur("voev")), ("aev", p.cur("aev"))]

            def mm_gate(e, m=m, bank=bank):
                r = None
                for k in range(8):
                    r = e.matmul(bank[:, 0:GR], wgv[:, k, m * 128:(m + 1) * 128],
                                 xg[:, k * GR:(k + 1) * GR],
                                 start=(k == 0), stop=(k == 7), skip_group_check=True)
                return r
            p.on("tensor", mm_gate, waits=[("wld", t_wld), ("xl", t_xl)] + ev_w, inc="gmm")
            t_mm = p.cur("gmm")
            p.on("scalar", lambda e, m=m, bank=bank: e.activation(
                gateg[:, m * GR:(m + 1) * GR], bank[:, 0:GR], AF.Silu,
                bias=bgc[:, m:m + 1]),
                waits=[("gmm", t_mm), ("scl", t_scl)], inc="gev")
        t_gev = p.cur("gev")

        bank_xq = ps[6]
        ev_w = [("affd", p.cur("affd"))]

        def mm_xq(e, bank=bank_xq):
            r = None
            for k in range(8):
                r = e.matmul(bank[:, 0:GR], win_sb[:, k * 128:(k + 1) * 128],
                             xg[:, k * GR:(k + 1) * GR],
                             start=(k == 0), stop=(k == 7), skip_group_check=True)
            return r
        p.on("tensor", mm_xq, waits=[("wld", t_wld), ("xl", t_xl), ("s3", p.cur("s3")),
                                     ("scl", t_scl)] + ev_w, inc="xqmm")
        t_mm = p.cur("xqmm")
        p.on("scalar", lambda e: e.activation(xq[:], bank_xq[:, 0:GR], AF.Silu,
                                              bias=af9[:, 0:1]),
             waits=[("xqmm", t_mm), ("scl", t_scl)], inc="xqev")
        t_xq = p.cur("xqev")

        def affines(v):
            v.tensor_scalar(qq[:], xq[:], af9[:, 1:2], af9[:, 2:3], ALU.mult, ALU.add)
            v.tensor_scalar(qk[:], xq[:], af9[:, 3:4], af9[:, 4:5], ALU.mult, ALU.add)
            v.tensor_scalar(lq[:], xq[:], af9[:, 5:6], af9[:, 6:7], ALU.mult, ALU.add)
            return v.tensor_scalar(lkpre[:], xq[:], af9[:, 7:8], af9[:, 8:9],
                                   ALU.mult, ALU.add)
        p.on("vector", affines, waits=[("xqev", t_xq), ("lkd", p.cur("lkd")),
                                       ("amm", p.cur("amm")), ("vomm", p.cur("vomm"))],
             inc="affd")
        t_aff = p.cur("affd")

        pbf = ps[7][:, 0:64].bitcast(BF16)
        for tj in range(2):
            tt = p.on("tensor", lambda e, tj=tj: e.transpose(
                pbf, lkpre[:, tj * 128:(tj + 1) * 128], identb[:]),
                waits=[("affd", t_aff), ("lkd", p.cur("lkd")), ("s3", p.cur("s3")),
                       id_ready], inc="lkt")
            p.on("vector", lambda e, tj=tj: e.tensor_copy(lkT[tj][:], pbf),
                 waits=[("lkt", tt)], inc="lkd")
        t_lkd = p.cur("lkd")

        def mm_attn(e):
            e.matmul(ps[4][:, 0:GR], qk[:, 0:128], qq[:, 0:GR],
                     start=True, stop=True, skip_group_check=True)
            return e.matmul(ps[5][:, 0:128], qk[:, 128:GR], qq[:, 128:GR],
                            start=True, stop=True, skip_group_check=True)
        p.on("tensor", mm_attn, waits=[("affd", t_aff), ("gev", t_gev),
                                       ("asqd", p.cur("asqd"))], inc="amm")
        t_amm = p.cur("amm")
        p.on("scalar", lambda e: e.activation(ta[:, 0:GR], ps[4][:, 0:GR], AF.Relu,
                                              scale=sc_attn[:, 0:1]),
             waits=[("amm", t_amm), ("asq1", p.cur("asq1"))], inc="aev")
        p.on("scalar", lambda e: e.activation(ta[:, GR:GR + 128], ps[5][:, 0:128],
                                              AF.Relu, scale=sc_attn[:, 0:1]),
             waits=[("amm", t_amm)], inc="aev")
        t_aev = p.cur("aev")
        p.on("vector", lambda e: e.tensor_mul(asq[:], ta[:], ta[:]),
             waits=[("aev", t_aev), ("vomm", p.cur("vomm"))], inc="asq1")
        t_asq1 = p.cur("asq1")

        def mask(gp):
            gp.affine_select(asq[:, 0:GR], asq[:, 0:GR], [[1, GR]], ALU.is_ge, 0.0,
                             base=0, channel_multiplier=-1)
            gp.engine_nop()
            return gp.affine_select(asq[:, GR:GR + 128], asq[:, GR:GR + 128],
                                    [[1, 128]], ALU.is_ge, 0.0,
                                    base=0, channel_multiplier=-1)
        p.on("gpsimd", mask, waits=[("asq1", t_asq1)], inc="asqd")
        t_asqd = p.cur("asqd")

        for et in range(4):
            def mm_kv(e, et=et):
                e.matmul(ps[et][:], lkT[0][:], vg[0][:, et * 512:(et + 1) * 512],
                         start=True, stop=False, skip_group_check=True)
                return e.matmul(ps[et][:], lkT[1][:], vg[1][:, et * 512:(et + 1) * 512],
                                start=False, stop=True, skip_group_check=True)
            p.on("tensor", mm_kv, waits=[("lkd", t_lkd), ("vev", t_vev),
                                         ("kvup", p.cur("kvup"))], inc="kvmm")
        t_kvmm = p.cur("kvmm")
        for ec in range(16):
            bank = ps[4 + (ec % 2)]
            ev_w = [("voev", max(0, p.cur("voev") - 1))]

            def mm_vo(e, ec=ec, bank=bank):
                e.matmul(bank[:, 0:GR], vg[0][:, ec * 128:(ec + 1) * 128],
                         asq[:, 0:GR], start=True, stop=False, skip_group_check=True)
                e.matmul(bank[:, 128:GR], vg[1][:, ec * 128:(ec + 1) * 128],
                         asq[:, GR:GR + 128], start=False, stop=False,
                         skip_group_check=True)
                return e.matmul(bank[:, 0:GR], kvb[:, ec * 128:(ec + 1) * 128],
                                lq[:, 0:GR], start=False, stop=True,
                                skip_group_check=True)
            p.on("tensor", mm_vo, waits=[("asqd", t_asqd), ("kvup", p.cur("kvup"))] + ev_w,
                 inc="vomm")
            t_vomm = p.cur("vomm")
            p.on("vector", lambda e, ec=ec, bank=bank: e.tensor_mul(
                og[:, ec * GR:(ec + 1) * GR], bank[:, 0:GR],
                gateg[:, ec * GR:(ec + 1) * GR]),
                waits=[("vomm", t_vomm), ("gev", t_gev), ("womm", p.cur("womm"))],
                inc="voev")
        t_voev = p.cur("voev")

        def kv_update(v):
            for et in range(4):
                v.tensor_add(kvf[:, et * 512:(et + 1) * 512], kvf[:, et * 512:(et + 1) * 512],
                             ps[et][:])
            v.drain()
            return v.tensor_copy(kvb[:], kvf[:])
        p.on("vector", kv_update, waits=[("kvmm", t_kvmm), ("vomm", p.cur("vomm"))],
             inc="kvup")

        for tj in range(2):
            rj = G0 + tj * 128
            ywaits = [("voev", t_voev), ("wld", t_wld), ("s3", p.cur("s3"))]

            def mm_y(e, tj=tj):
                r = None
                for q in range(2):
                    for ec in range(16):
                        e.matmul(ps[6 + q][:], og[:, ec * GR + tj * 128:ec * GR + (tj + 1) * 128],
                                 wov[:, ec, q * 512:(q + 1) * 512],
                                 start=(ec == 0), stop=False, skip_group_check=True)
                    r = e.matmul(ps[6 + q][:], ones1[:], bout_row[0:1, q * 512:(q + 1) * 512],
                                 start=False, stop=True, skip_group_check=True)
                return r
            p.on("tensor", mm_y, waits=ywaits, inc="womm")
            t_w = p.cur("womm")

            def ev_y(v):
                v.tensor_copy(s3t.ysb[:, 0:512], ps[6][:])
                return v.tensor_copy(s3t.ysb[:, 512:1024], ps[7][:])
            p.on("vector", ev_y, waits=[("womm", t_w)], inc="yev")
            t_yev = p.cur("yev")
            emit_s3(p, lns, s3t, resin, resout, lnxout, rj, [("yev", t_yev)])
        p.on("tensor", lambda e: e.nop(),
             waits=[("vmm", p.cur("vmm"))], inc="xcons")

    t_k = p.cur("kvup")
    p.on("sync", lambda e: e.dma_start(out=kvout, in_=kvf[:]),
         waits=[("kvup", t_k)], inc="st", inc_by=16)
    p.on("sync", lambda e: e.nop(), waits=[("st", p.cur("st"))])
    return finish(nc, p, es)


def host_prep_att0(inp):
    vecs9 = np.stack([inp["L0_attn_bin_"], inp["L0_attn_g_qq"], inp["L0_attn_b_qq"],
                      inp["L0_attn_g_qk"], inp["L0_attn_b_qk"], inp["L0_attn_g_lq"],
                      inp["L0_attn_b_lq"], inp["L0_attn_g_lk"], inp["L0_attn_b_lk"]])
    return dict(Wv=inp["L0_attn_Wv"], Wg=inp["L0_attn_Wg"], Win=inp["L0_attn_Win"],
                Wout=inp["L0_attn_Wout"], bv=inp["L0_attn_bv"], bg=inp["L0_attn_bg"],
                bout=inp["L0_attn_bout"], vecs9=np.ascontiguousarray(vecs9))


def build_att1(T, W=256):
    nwin = T // W
    WX = W + 128
    nc = bass.Bass("TRN2", target_bir_lowering=False, debug=False)
    lnxp = nc.dram_tensor("lnxp", [T + 129, D], BF16, kind="ExternalInput").ap()
    resin = nc.dram_tensor("resin", [T, D], F32, kind="ExternalInput").ap()
    wkT = nc.dram_tensor("wkT", [D, D], F32, kind="ExternalInput").ap()
    wvT = nc.dram_tensor("wvT", [D, D], F32, kind="ExternalInput").ap()
    wrT = nc.dram_tensor("wrT", [D, D], F32, kind="ExternalInput").ap()
    woT = nc.dram_tensor("woT", [D, D], F32, kind="ExternalInput").ap()
    tvec5 = nc.dram_tensor("tvec5", [5, D], F32, kind="ExternalInput").ap()
    resout = nc.dram_tensor("resout", [T, D], F32, kind="ExternalOutput").ap()
    lnxout = nc.dram_tensor("lnxout", [T, D], BF16, kind="ExternalOutput").ap()

    es = ExitStack()
    sb = lambda n, shp, dt: es.enter_context(nc.sbuf_tensor(n, shp, dt))
    wk_sb = sb("wk_sb", [128, 8 * D], BF16)
    wv_sb = sb("wv_sb", [128, 8 * D], BF16)
    wr_sb = sb("wr_sb", [128, 8 * D], BF16)
    wo_sb = sb("wo_sb", [128, 8 * D], BF16)
    identf = sb("identf", [128, 128], F32)
    xw = sb("xw", [128, 8 * WX], BF16)
    mixb = sb("mixb", [128, 8 * W], BF16)
    ek = sb("ek", [128, 8 * W], F32)
    vv = sb("vv", [128, 8 * W], F32)
    sr = sb("srb", [128, 8 * W], BF16)
    rec = sb("rec", [128, 8 * W], F32)
    abuf = sb("abuf", [128, 8 * (W + 1)], F32)
    bbuf = sb("bbuf", [128, 8 * (W + 1)], F32)
    rwkv = sb("rwkv", [128, 8 * W], BF16)
    ewt = sb("ewt", [128, 8 * W], F32)
    acar = sb("acar", [128, 8], F32)
    bcar = sb("bcar", [128, 8], F32)
    ps = alloc_psum(nc, es, 8)
    lns = LNScratch(nc, es)
    s3t = S3Tiles(nc, es)
    p = Prog(nc)

    emit_identity(p, identf[:], 128)
    id_ready = ("ident", p.cur("ident"))
    tdc = emit_scale_cols(p, nc, es, "tdc", tvec5[0], 8, identf, id_ready, ps[6])
    tfc = emit_scale_cols(p, nc, es, "tfc", tvec5[1], 8, identf, id_ready, ps[6])
    tmkc = emit_scale_cols(p, nc, es, "tmkc", tvec5[2], 8, identf, id_ready, ps[6])
    tmvc = emit_scale_cols(p, nc, es, "tmvc", tvec5[3], 8, identf, id_ready, ps[6])
    tmrc = emit_scale_cols(p, nc, es, "tmrc", tvec5[4], 8, identf, id_ready, ps[6])
    t_scl = p.cur("scl")
    ewc = sb("ewc", [128, 8], F32)
    euc = sb("euc", [128, 8], F32)
    omk = sb("omk", [128, 8], F32)
    omv = sb("omv", [128, 8], F32)
    omr = sb("omr", [128, 8], F32)

    def mk_consts_se(e):
        e.activation(ewc[:], tdc[:], AF.Exp)
        e.drain()
        e.activation(ewc[:], ewc[:], AF.Exp, scale=-1.0)
        return e.activation(euc[:], tfc[:], AF.Exp)
    p.on("scalar", mk_consts_se, waits=[("scl", t_scl)], inc="cst")
    t_c1 = p.cur("cst")

    def mk_consts_dve(v):
        v.tensor_scalar(omk[:], tmkc[:], -1.0, 1.0, ALU.mult, ALU.add)
        v.tensor_scalar(omv[:], tmvc[:], -1.0, 1.0, ALU.mult, ALU.add)
        v.tensor_scalar(omr[:], tmrc[:], -1.0, 1.0, ALU.mult, ALU.add)
        v.memset(acar[:], 0.0)
        v.memset(bcar[:], 0.0)
        v.drain()
        for c in range(NCH):
            v.tensor_scalar(ewt[:, c * W:(c + 1) * W], ewt[:, c * W:(c + 1) * W],
                            ewc[:, c:c + 1], None, ALU.mult)
        return v.engine_nop()
    p.on("gpsimd", lambda e: e.memset(ewt[:], 1.0), inc="cst")
    t_c2 = p.cur("cst")
    p.on("vector", mk_consts_dve, waits=[("cst", t_c2), ("scl", t_scl)], inc="cst")
    t_cst = p.cur("cst")

    for nm, dram, dst in [("wk", wkT, wk_sb), ("wv", wvT, wv_sb),
                          ("wr", wrT, wr_sb), ("wo", woT, wo_sb)]:
        p.on("gpsimd", lambda e, dram=dram, dst=dst: e.dma_start(
            out=dst[:].rearrange("p (k e) -> p k e", k=8),
            in_=dram.rearrange("(k p) e -> p k e", p=128)), inc="wld", inc_by=16)
    t_wld = p.cur("wld")
    wkv = wk_sb[:].rearrange("p (k e) -> p k e", k=8)
    wvv = wv_sb[:].rearrange("p (k e) -> p k e", k=8)
    wrv = wr_sb[:].rearrange("p (k e) -> p k e", k=8)
    wov = wo_sb[:].rearrange("p (k e) -> p k e", k=8)
    AW = W + 1

    for w in range(nwin):
        R0 = w * W
        xwaits = [("mixd", p.cur("mixd"))] if w else []
        for c in range(NCH):
            p.on("sync", lambda e, c=c, R0=R0: e.dma_start(
                out=xw[:, c * WX:c * WX + WX],
                in_=lnxp[R0:R0 + WX, c * 128:(c + 1) * 128], transpose=True),
                waits=xwaits, inc="xl", inc_by=16)
        t_xl = p.cur("xl")

        for pi, (mixc, omc, wmat, evn) in enumerate([
                (tmkc, omk, wkv, "ek"), (tmvc, omv, wvv, "vv"), (tmrc, omr, wrv, "sr")]):
            def mixes(v, mixc=mixc, omc=omc):
                for c in range(NCH):
                    v.tensor_scalar(mixb[:, c * W:(c + 1) * W],
                                    xw[:, c * WX + 1:c * WX + 1 + W],
                                    mixc[:, c:c + 1], None, ALU.mult)
                v.drain()
                r = None
                for c in range(NCH):
                    r = v.scalar_tensor_tensor(mixb[:, c * W:(c + 1) * W],
                                               xw[:, c * WX:c * WX + W],
                                               omc[:, c:c + 1],
                                               mixb[:, c * W:(c + 1) * W],
                                               ALU.mult, ALU.add)
                return r
            p.on("vector", mixes, waits=[("xl", t_xl), ("cst", t_cst),
                                         ("pmm", p.cur("pmm"))], inc="mixd")
            t_mix = p.cur("mixd")
            for m in range(NCH):
                bank = ps[m % 2]
                ev_w = [("pev", max(0, p.cur("pev") - 1))]
                if pi == 0 and m < 2 and w > 0:
                    pass

                def mm_p(e, m=m, bank=bank, wmat=wmat):
                    r = None
                    for k in range(NCH):
                        r = e.matmul(bank[:, 0:W], wmat[:, k, m * 128:(m + 1) * 128],
                                     mixb[:, k * W:(k + 1) * W],
                                     start=(k == 0), stop=(k == 7),
                                     skip_group_check=True)
                    return r
                p.on("tensor", mm_p, waits=[("wld", t_wld), ("mixd", t_mix)] + ev_w,
                     inc="pmm")
                t_mm = p.cur("pmm")
                if evn == "ek":
                    evw = [("recd", p.cur("recd"))] if (w and m < 2) else []
                    p.on("scalar", lambda e, m=m, bank=bank: e.activation(
                        ek[:, m * W:(m + 1) * W], bank[:, 0:W], AF.Exp),
                        waits=[("pmm", t_mm)] + evw, inc="pev")
                elif evn == "vv":
                    evw = [("rwkd", p.cur("rwkd"))] if (w and m < 2) else []
                    p.on("scalar", lambda e, m=m, bank=bank: e.copy(
                        vv[:, m * W:(m + 1) * W], bank[:, 0:W]),
                        waits=[("pmm", t_mm)] + evw, inc="pev")
                else:
                    evw = [("rwkd", p.cur("rwkd"))] if (w and m < 2) else []
                    p.on("scalar", lambda e, m=m, bank=bank: e.activation(
                        sr[:, m * W:(m + 1) * W], bank[:, 0:W], AF.Sigmoid),
                        waits=[("pmm", t_mm)] + evw, inc="pev")
        t_pev = p.cur("pev")

        def scan_block(v):
            for c in range(NCH):
                v.tensor_mul(vv[:, c * W:(c + 1) * W], vv[:, c * W:(c + 1) * W],
                             ek[:, c * W:(c + 1) * W])
            v.drain()
            for c in range(NCH):
                v.tensor_copy(abuf[:, c * AW:c * AW + 1], acar[:, c:c + 1])
                v.tensor_copy(bbuf[:, c * AW:c * AW + 1], bcar[:, c:c + 1])
            v.drain()
            for c in range(NCH):
                v.tensor_tensor_scan(abuf[:, c * AW + 1:c * AW + 1 + W],
                                     ewt[:, c * W:(c + 1) * W],
                                     vv[:, c * W:(c + 1) * W],
                                     acar[:, c:c + 1], ALU.mult, ALU.add)
                v.tensor_tensor_scan(bbuf[:, c * AW + 1:c * AW + 1 + W],
                                     ewt[:, c * W:(c + 1) * W],
                                     ek[:, c * W:(c + 1) * W],
                                     bcar[:, c:c + 1], ALU.mult, ALU.add)
            v.drain()
            for c in range(NCH):
                v.tensor_copy(acar[:, c:c + 1], abuf[:, c * AW + W:c * AW + W + 1])
                v.tensor_copy(bcar[:, c:c + 1], bbuf[:, c * AW + W:c * AW + W + 1])
            v.drain()
            r = None
            for c in range(NCH):
                v.scalar_tensor_tensor(vv[:, c * W:(c + 1) * W],
                                       vv[:, c * W:(c + 1) * W], euc[:, c:c + 1],
                                       abuf[:, c * AW:c * AW + W], ALU.mult, ALU.add)
                r = v.scalar_tensor_tensor(ek[:, c * W:(c + 1) * W],
                                           ek[:, c * W:(c + 1) * W], euc[:, c:c + 1],
                                           bbuf[:, c * AW:c * AW + W], ALU.mult, ALU.add)
            return r
        p.on("vector", scan_block, waits=[("pev", t_pev)], inc="dend")
        t_den = p.cur("dend")

        def div_se(e):
            for c in range(NCH):
                e.activation(rec[:, c * W:(c + 1) * W], ek[:, c * W:(c + 1) * W], AF.Ln)
            e.drain()
            r = None
            for c in range(NCH):
                r = e.activation(rec[:, c * W:(c + 1) * W], rec[:, c * W:(c + 1) * W],
                                 AF.Exp, scale=-1.0)
            return r
        p.on("scalar", div_se, waits=[("dend", t_den)], inc="recd")
        t_rec = p.cur("recd")

        def rwkv_dve(v):
            for c in range(NCH):
                v.tensor_mul(vv[:, c * W:(c + 1) * W], vv[:, c * W:(c + 1) * W],
                             rec[:, c * W:(c + 1) * W])
            v.drain()
            r = None
            for c in range(NCH):
                r = v.tensor_mul(rwkv[:, c * W:(c + 1) * W], vv[:, c * W:(c + 1) * W],
                                 sr[:, c * W:(c + 1) * W])
            return r
        p.on("vector", rwkv_dve, waits=[("recd", t_rec), ("womm", p.cur("womm"))],
             inc="rwkd")
        t_rwk = p.cur("rwkd")

        for j in range(W // 128):
            rj = R0 + j * 128
            ywaits = [("rwkd", t_rwk), ("wld", t_wld), ("s3", p.cur("s3")),
                      ("scl", t_scl)]

            def mm_y(e, j=j):
                r = None
                for q in range(2):
                    for k in range(NCH):
                        r = e.matmul(ps[6 + q][:],
                                     rwkv[:, k * W + j * 128:k * W + (j + 1) * 128],
                                     wov[:, k, q * 512:(q + 1) * 512],
                                     start=(k == 0), stop=(k == 7),
                                     skip_group_check=True)
                return r
            p.on("tensor", mm_y, waits=ywaits, inc="womm")
            t_w = p.cur("womm")

            def ev_y(v):
                v.tensor_copy(s3t.ysb[:, 0:512], ps[6][:])
                return v.tensor_copy(s3t.ysb[:, 512:1024], ps[7][:])
            p.on("vector", ev_y, waits=[("womm", t_w)], inc="yev")
            t_yev = p.cur("yev")
            emit_s3(p, lns, s3t, resin, resout, lnxout, rj, [("yev", t_yev)])

    p.on("sync", lambda e: e.nop(), waits=[("st", p.cur("st"))])
    return finish(nc, p, es)


def host_prep_att1(inp):
    C = np.ascontiguousarray
    tvec5 = np.stack([inp["L1_attn_time_decay"], inp["L1_attn_time_first"],
                      inp["L1_attn_time_mix_k"], inp["L1_attn_time_mix_v"],
                      inp["L1_attn_time_mix_r"]])
    return dict(wkT=C(inp["L1_attn_Wk"].T), wvT=C(inp["L1_attn_Wv"].T),
                wrT=C(inp["L1_attn_Wr"].T), woT=C(inp["L1_attn_Wo"].T),
                tvec5=C(tvec5))


def build_ffn1(T, PT=1024):
    nc = bass.Bass("TRN2", target_bir_lowering=False, debug=False)
    lnx = nc.dram_tensor("lnx", [T, D], BF16, kind="ExternalInput").ap()
    resin = nc.dram_tensor("resin", [T, D], F32, kind="ExternalInput").ap()
    Win = nc.dram_tensor("Win", [D, 64], F32, kind="ExternalInput").ap()
    bin_ = nc.dram_tensor("bin_", [64], F32, kind="ExternalInput").ap()
    W1 = nc.dram_tensor("W1", [64, 16], F32, kind="ExternalInput").ap()
    b1 = nc.dram_tensor("b1", [64, 16], F32, kind="ExternalInput").ap()
    W2 = nc.dram_tensor("W2", [64, 16], F32, kind="ExternalInput").ap()
    b2 = nc.dram_tensor("b2", [64], F32, kind="ExternalInput").ap()
    Wout = nc.dram_tensor("Wout", [64, D], F32, kind="ExternalInput").ap()
    bout = nc.dram_tensor("bout", [D], F32, kind="ExternalInput").ap()
    out = nc.dram_tensor("out", [T, D], F32, kind="ExternalOutput").ap()

    es = ExitStack()
    sb = lambda n, shp, dt: es.enter_context(nc.sbuf_tensor(n, shp, dt))
    win_sb = sb("win_sb", [128, 8 * 64], BF16)
    wout_sb = sb("wout_sb", [128, 1024], BF16)
    xfm = sb("xfm", [128, 8 * PT], BF16)
    s_sb = sb("s_sb", [128, PT // 2], F32)
    tbuf = sb("tbuf", [128, PT // 2], F32)
    gacc = sb("gacc", [128, PT // 2], F32)
    gaccb = sb("gaccb", [128, PT // 2], BF16)
    w1c = sb("w1c", [128, 16], F32)
    b1c = sb("b1c", [128, 16], F32)
    w2c = sb("w2c", [128, 16], F32)
    binc = sb("binc", [128, 1], F32)
    b2col = sb("b2col", [64, 1], F32)
    b2colb = sb("b2colb", [64, 1], BF16)
    brow = sb("brow", [1, 1024], F32)
    boutrow = sb("boutrow", [1, 1024], F32)
    rows1 = sb("rows1", [1, 64], F32)
    ones1 = sb("ones1", [1, 128], F32)
    identf = sb("identf", [128, 128], F32)
    ps = alloc_psum(nc, es, 8)
    lns = LNScratch(nc, es)
    s3t = S3Tiles(nc, es)
    p = Prog(nc)
    W2T = PT // 2

    emit_identity(p, identf[:], 128)
    id_ready = ("ident", p.cur("ident"))
    p.on("gpsimd", lambda e: e.memset(ones1[:], 1.0), inc="ident")
    p.on("gpsimd", lambda e: e.dma_start(
        out=win_sb[:].rearrange("p (k e) -> p k e", k=8),
        in_=Win.rearrange("(k p) e -> p k e", p=128)), inc="wld", inc_by=16)
    p.on("gpsimd", lambda e: e.dma_start(out=wout_sb[0:64, :], in_=Wout), inc="wld", inc_by=16)
    p.on("gpsimd", lambda e: e.dma_start(out=wout_sb[64:128, :], in_=Wout), inc="wld", inc_by=16)
    for dst, src in [(w1c, W1), (b1c, b1), (w2c, W2)]:
        p.on("sync", lambda e, dst=dst, src=src: e.dma_start(out=dst[0:64, :], in_=src),
             inc="wld", inc_by=16)
        p.on("sync", lambda e, dst=dst, src=src: e.dma_start(out=dst[64:128, :], in_=src),
             inc="wld", inc_by=16)
    p.on("sync", lambda e: e.dma_start(out=rows1[:], in_=bin_.unsqueeze(0)),
         inc="wld", inc_by=16)
    p.on("sync", lambda e: e.dma_start(out=boutrow[:], in_=bout.unsqueeze(0)),
         inc="wld", inc_by=16)
    t_wld = p.cur("wld")
    tt = p.on("tensor", lambda e: e.transpose(ps[6][0:64, 0:1], rows1[:],
                                              identf[0:1, 0:1]),
              waits=[("wld", t_wld), id_ready], inc="setup")
    p.on("vector", lambda e: (e.tensor_copy(binc[0:64, :], ps[6][0:64, 0:1]),
                              e.tensor_copy(binc[64:128, :], ps[6][0:64, 0:1]))[-1],
         waits=[("setup", tt)], inc="setup")
    t_b = p.on("sync", lambda e: e.dma_start(out=rows1[0:1, 0:64], in_=b2.unsqueeze(0)),
               waits=[("setup", p.cur("setup"))], inc="wld", inc_by=16)
    tt2 = p.on("tensor", lambda e: e.transpose(ps[6][0:64, 0:1], rows1[0:1, 0:64],
                                               identf[0:1, 0:1]),
               waits=[("wld", t_b), ("setup", p.cur("setup"))], inc="setup")
    p.on("vector", lambda e: (e.tensor_copy(b2col[:], ps[6][0:64, 0:1]),
                              e.drain(),
                              e.tensor_copy(b2colb[:], b2col[:]))[-1],
         waits=[("setup", tt2)], inc="setup")
    t_su = p.cur("setup")
    for q in range(2):
        p.on("tensor", lambda e, q=q: e.matmul(
            ps[6 + q][0:1, :], b2colb[:], wout_sb[0:64, q * 512:(q + 1) * 512],
            start=True, stop=True, skip_group_check=True),
            waits=[("setup", t_su), ("wld", t_wld)], inc="setup")
    t_su2 = p.cur("setup")
    p.on("vector", lambda e: (e.tensor_add(brow[0:1, 0:512], ps[6][0:1, :], boutrow[0:1, 0:512]),
                              e.tensor_add(brow[0:1, 512:1024], ps[7][0:1, :],
                                           boutrow[0:1, 512:1024]))[-1],
         waits=[("setup", t_su2)], inc="setup")
    t_setup = p.cur("setup")

    winv = win_sb[:].rearrange("p (k e) -> p k e", k=8)
    npass = T // PT
    for ip in range(npass):
        r0 = ip * PT
        xw = [("smm", p.cur("smm"))] if ip else []
        for c in range(NCH):
            p.on("sync", lambda e, c=c, r0=r0: e.dma_start(
                out=xfm[:, c * PT:(c + 1) * PT],
                in_=lnx[r0:r0 + PT, c * 128:(c + 1) * 128], transpose=True),
                waits=xw, inc="xl", inc_by=16)
        t_xl = p.cur("xl")

        def mm_s(e):
            r = None
            for half in range(2):
                o = ps[0][half * 64:(half + 1) * 64, 0:W2T]
                for k in range(NCH):
                    r = e.matmul(o, winv[:, k, :],
                                 xfm[:, k * PT + half * W2T:k * PT + (half + 1) * W2T],
                                 start=(k == 0), stop=(k == 7), skip_group_check=True)
            return r
        p.on("tensor", mm_s, waits=[("wld", t_wld), ("xl", t_xl),
                                    ("gd", p.cur("gd"))], inc="smm")
        t_smm = p.cur("smm")
        p.on("vector", lambda e: e.tensor_scalar(s_sb[:], ps[0][:, 0:W2T], binc[:, 0:1],
                                                 None, ALU.add),
             waits=[("smm", t_smm), ("setup", t_setup)], inc="sev")
        t_sev = p.cur("sev")

        def g_chain(v):
            v.memset(gacc[:], 0.0)
            for i in range(16):
                v.drain()
                v.tensor_scalar(tbuf[:], s_sb[:], w1c[:, i:i + 1], b1c[:, i:i + 1],
                                ALU.mult, ALU.add)
                v.drain()
                v.tensor_scalar_max(tbuf[:], tbuf[:], 0.0)
                v.drain()
                v.scalar_tensor_tensor(gacc[:], tbuf[:], w2c[:, i:i + 1], gacc[:],
                                       ALU.mult, ALU.add)
            v.drain()
            return v.tensor_copy(gaccb[:], gacc[:])
        p.on("vector", g_chain, waits=[("sev", t_sev), ("omm", p.cur("omm"))], inc="gd")
        t_gd = p.cur("gd")

        for j in range(PT // 128):
            half = j // (W2T // 128)
            jj = j % (W2T // 128)
            rj = r0 + j * 128
            ywaits = [("gd", t_gd), ("s3", p.cur("s3")), ("setup", t_setup)]

            def mm_o(e, half=half, jj=jj):
                r = None
                for q in range(2):
                    lh = gaccb[half * 64:(half + 1) * 64, jj * 128:(jj + 1) * 128]
                    e.matmul(ps[6 + q][:], lh,
                             wout_sb[half * 64:(half + 1) * 64, q * 512:(q + 1) * 512],
                             start=True, stop=False, skip_group_check=True)
                    r = e.matmul(ps[6 + q][:], ones1[:], brow[0:1, q * 512:(q + 1) * 512],
                                 start=False, stop=True, skip_group_check=True)
                return r
            p.on("tensor", mm_o, waits=ywaits, inc="omm")
            t_o = p.cur("omm")

            def ev_y(v):
                v.tensor_copy(s3t.ysb[:, 0:512], ps[6][:])
                return v.tensor_copy(s3t.ysb[:, 512:1024], ps[7][:])
            p.on("vector", ev_y, waits=[("omm", t_o)], inc="yev")
            t_yev = p.cur("yev")
            emit_s3(p, lns, s3t, resin, out, None, rj, [("yev", t_yev)], final=True)

    p.on("sync", lambda e: e.nop(), waits=[("st", p.cur("st"))])
    return finish(nc, p, es)


def host_prep_ffn1(inp):
    return dict(Win=inp["L1_ffn_Win"], bin_=inp["L1_ffn_bin_"], W1=inp["L1_ffn_W1"],
                b1=inp["L1_ffn_b1"], W2=inp["L1_ffn_W2"], b2=inp["L1_ffn_b2"],
                Wout=inp["L1_ffn_Wout"], bout=inp["L1_ffn_bout"])




TH = 4096


def barrier(p, name="stg"):
    t_st = p.cur("st")
    for e in ENGINES:
        w = [("st", t_st)] if e == "sync" else []
        p.on(e, lambda en: en.nop(), waits=w, inc=name)
    tgt = p.cur(name)
    for e in ENGINES:
        p.on(e, lambda en: en.nop(), waits=[(name, tgt)])
    return tgt


class Mega:

    def __init__(self, nc, es, p):
        self.nc, self.es, self.p = nc, es, p
        sb = lambda n, shp, dt: es.enter_context(nc.sbuf_tensor(n, shp, dt))
        self.identf = sb("identf", [128, 128], F32)
        self.identb = sb("identb", [128, 128], BF16)
        self.ones1 = sb("ones1", [1, 128], F32)
        self.ps = alloc_psum(nc, es, 8)
        self.lns = LNScratch(nc, es)
        self.s3t = S3Tiles(nc, es)
        self.regs = {}

    def setup(self, t_role, t_prole, t_nonce, t_flag):
        self.t_nonce = t_nonce
        self.t_flag = t_flag
        p = self.p
        emit_identity(p, self.identf[:], 128)
        p.on("gpsimd", lambda e: (e.memset(self.identb[:], 1.0), e.engine_nop(),
                                  e.affine_select(self.identb[:], self.identb[:],
                                                  [[1, 128]], ALU.is_equal, 0.0,
                                                  base=0, channel_multiplier=-1))[-1],
             inc="ident")
        p.on("gpsimd", lambda e: e.memset(self.ones1[:], 1.0), inc="ident")
        regs = self.regs

        def load_regs(g):
            regs["rv"] = g.bass.values_load(
                t_role[:], engines=[EngineType.Pool], min_val=0, max_val=1,
                skip_runtime_bounds_check=True)
            regs["prv"] = g.bass.values_load(
                t_prole[:], engines=[EngineType.Pool], min_val=0, max_val=1,
                skip_runtime_bounds_check=True)
            nv = g.register("nonce_reg")
            h = nv.__enter__()
            g.reg_load(h, t_nonce[:])
            regs["nv"] = h
            return g.engine_nop()
        p.on("gpsimd", load_regs, waits=[("iold", p.cur("iold"))], inc="ident")
        return ("ident", p.cur("ident"))


def emit_exchange(mg, pref, send_sbuf, sh_data, sh_flag, flag_idx, recv_sbuf,
                  rows, ready):
    p, regs = mg.p, mg.regs

    def go(g):
        rv, prv, nv = regs["rv"], regs["prv"], regs["nv"]
        sem = p.sems[pref + "_x"]
        g.dma_start(out=sh_data.ap()[bass.ts(rv, rows)], in_=send_sbuf).then_inc(sem, 16)
        g.wait_ge(sem, 16)
        g.dma_start(out=sh_flag.ap()[bass.ts(rv, 1), flag_idx:flag_idx + 1],
                    in_=mg.t_nonce[:]).then_inc(sem, 16)
        g.wait_ge(sem, 32)
        with g.register(pref + "_c") as c, g.register(pref + "_t") as t, \
                g.register(pref + "_f") as f:
            g.reg_mov(c, 1)
            g.reg_mov(t, 32)
            with g.While(c):
                g.reg_add(t, t, 16)
                g.dma_start(out=mg.t_flag[:],
                            in_=sh_flag.ap()[bass.ts(prv, 1), flag_idx:flag_idx + 1]
                            ).then_inc(sem, 16)
                g.wait_ge(sem, t)
                g.reg_load(f, mg.t_flag[:])
                g.reg_alu(c, f, nv, ALU.bitwise_xor)
            g.reg_add(t, t, 16)
            g.dma_start(out=recv_sbuf, in_=sh_data.ap()[bass.ts(prv, rows)]
                        ).then_inc(sem, 16)
            g.wait_ge(sem, t)
        return g.engine_nop()
    p.touch(pref + "_x")
    p.on("gpsimd", go, waits=list(ready), inc=pref + "_d")
    return (pref + "_d", p.cur(pref + "_d"))


def m_ln0(mg, xin, lnx0):
    nc, es, p = mg.nc, mg.es, mg.p
    with ExitStack() as ses:
        xt = [ses.enter_context(nc.sbuf_tensor(f"l0xt{i}", [128, D], F32)) for i in range(2)]
        ot = [ses.enter_context(nc.sbuf_tensor(f"l0ot{i}", [128, D], BF16)) for i in range(2)]
        for i in range(TH // 128):
            b = i % 2
            waits = [("ln_done", i - 1)] if i >= 2 else []
            t_ld = p.on("sync", lambda e, i=i, b=b: e.dma_start(
                out=xt[b][:], in_=xin[i * 128:(i + 1) * 128, :]),
                waits=waits, inc="ld", inc_by=16)
            ready = [("ld", t_ld)]
            if i >= 2:
                ready.append(("st", p.cur("st") - 16))
            emit_ln(p, mg.lns, xt[b][:], ot[b][:], ready, "ln_done")
            t_done = p.cur("ln_done")
            p.on("sync", lambda e, i=i, b=b: e.dma_start(
                out=lnx0[i * 128:(i + 1) * 128, :], in_=ot[b][:]),
                waits=[("ln_done", t_done)], inc="st", inc_by=16)


def m_att0(mg, io, id_ready, x1):
    nc, es, p, ps = mg.nc, mg.es, mg.p, mg.ps
    GR = 256
    NG = TH // GR
    lnx0 = io["lnx0"]; vdram = io["vdram"]
    oes = ExitStack()
    sb = lambda n, shp, dt: oes.enter_context(nc.sbuf_tensor(n, shp, dt))
    kvf = sb("a0kvf", [128, 2048], F32)
    lkTall = sb("a0lkT", [128, NG * 2 * 128], BF16)
    win_sb = sb("a0win", [128, 8 * 128], BF16)
    af9 = emit_scale_cols(p, nc, oes, "a0af9", io["vecs9"].flatten(), 9,
                          mg.identf, id_ready, ps[6])
    t_scl0 = p.cur("scl")
    xg = sb("a0xg", [128, 8 * GR], BF16)
    vg = [sb(f"a0vg{t}", [128, 2048], BF16) for t in range(2)]
    xq = sb("a0xq", [128, GR], F32)
    lkpre = sb("a0lkpre", [128, GR], BF16)
    bv_row = sb("a0bvr", [1, 2048], F32)

    p.on("gpsimd", lambda e: e.dma_start(
        out=win_sb[:].rearrange("p (k e) -> p k e", k=8),
        in_=io["Win"].rearrange("(k p) e -> p k e", p=128)), inc="a0wl", inc_by=16)
    p.on("sync", lambda e: e.dma_start(out=bv_row[:], in_=io["bv"].unsqueeze(0)),
         inc="a0wl", inc_by=16)
    p.on("vector", lambda e: e.memset(kvf[:], 0.0), inc="a0kvup")

    def emit_xq(t_wld, t_xl, t_scl_use=None):
        def mm_xq(e):
            r = None
            for k in range(8):
                r = e.matmul(ps[6][:, 0:GR], win_sb[:, k * 128:(k + 1) * 128],
                             xg[:, k * GR:(k + 1) * GR], start=(k == 0),
                             stop=(k == 7), skip_group_check=True)
            return r
        p.on("tensor", mm_xq, waits=[("a0wl", t_wld), ("a0xl", t_xl),
                                     ("scl", t_scl_use or t_scl0),
                                     ("yev", p.cur("yev")),
                                     ("a0xqev", p.cur("a0xqev"))], inc="a0xqmm")
        t_mm = p.cur("a0xqmm")
        p.on("scalar", lambda e: e.activation(xq[:], ps[6][:, 0:GR], AF.Silu,
                                              bias=af9[:, 0:1]),
             waits=[("a0xqmm", t_mm), ("a0affd", p.cur("a0affd"))], inc="a0xqev")
        return p.cur("a0xqev")

    def emit_xg(G0, gate_sem):
        xw = [(gate_sem, p.cur(gate_sem))]
        for c in range(NCH):
            p.on("sync", lambda e, c=c, G0=G0: e.dma_start(
                out=xg[:, c * GR:(c + 1) * GR],
                in_=lnx0[G0:G0 + GR, c * 128:(c + 1) * 128], transpose=True),
                waits=xw, inc="a0xl", inc_by=16)
        return p.cur("a0xl")

    with ExitStack() as ses:
        wv_sb = ses.enter_context(nc.sbuf_tensor("a0wv1", [128, 8 * 2048], BF16))
        wvv = wv_sb[:].rearrange("p (k e) -> p k e", k=8)
        p.on("gpsimd", lambda e: e.dma_start(
            out=wvv, in_=io["Wv"].rearrange("(k p) e -> p k e", p=128)),
            inc="a0wl", inc_by=16)
        t_wld = p.cur("a0wl")
        for g in range(NG):
            G0 = g * GR
            t_xl = emit_xg(G0, "a0xc1")
            t_xq = emit_xq(t_wld, t_xl)
            p.on("vector", lambda e: e.tensor_scalar(
                lkpre[:], xq[:], af9[:, 7:8], af9[:, 8:9], ALU.mult, ALU.add),
                waits=[("a0xqev", t_xq), ("a0lkt", p.cur("a0lkt"))], inc="a0affd")
            t_lkp = p.cur("a0affd")
            pf = ps[7][:, 0:128]
            for tj in range(2):
                tt = p.on("tensor", lambda e, tj=tj: e.matmul(
                    pf, lkpre[:, tj * 128:(tj + 1) * 128], mg.identb[:],
                    start=True, stop=True, skip_group_check=True),
                    waits=[("a0affd", t_lkp), ("a0lkd", p.cur("a0lkd")), id_ready],
                    inc="a0lkt")
                p.on("vector", lambda e, g=g, tj=tj: e.tensor_copy(
                    lkTall[:, (2 * g + tj) * 128:(2 * g + tj + 1) * 128], pf),
                    waits=[("a0lkt", tt)], inc="a0lkd")
            t_lkd = p.cur("a0lkd")
            for tj in range(2):
                for et in range(4):
                    bank = ps[2 * tj + (et % 2)]
                    ev_w = [("a0vev", p.cur("a0vev"))]

                    def mm_v(e, tj=tj, et=et, bank=bank):
                        for k in range(8):
                            e.matmul(bank[:], xg[:, k * GR + tj * 128:k * GR + (tj + 1) * 128],
                                     wvv[:, k, et * 512:(et + 1) * 512],
                                     start=(k == 0), stop=False, skip_group_check=True)
                        return e.matmul(bank[:], mg.ones1[:],
                                        bv_row[0:1, et * 512:(et + 1) * 512],
                                        start=False, stop=True, skip_group_check=True)
                    p.on("tensor", mm_v, waits=[("a0wl", t_wld), ("a0xl", t_xl),
                                                ("a0kvmm", p.cur("a0kvmm"))] + ev_w,
                         inc="a0vmm")
                    t_vm = p.cur("a0vmm")
                    p.on("scalar", lambda e, tj=tj, et=et, bank=bank: e.activation(
                        vg[tj][:, et * 512:(et + 1) * 512], bank[:], AF.Silu),
                        waits=[("a0vmm", t_vm), ("a0vst", p.cur("a0vst"))],
                        inc="a0vev")
                t_vev = p.cur("a0vev")
                p.on("sync", lambda e, G0=G0, tj=tj: e.dma_start(
                    out=vdram[G0 + tj * 128:G0 + (tj + 1) * 128, :], in_=vg[tj][:]),
                    waits=[("a0vev", t_vev)], inc="a0vst", inc_by=16)
            t_vev = p.cur("a0vev")
            for half in range(2):
                for et2 in range(2):
                    et = 2 * half + et2

                    def mm_kv(e, g=g, et=et, et2=et2):
                        e.matmul(ps[4 + et2][:], lkTall[:, 2 * g * 128:(2 * g + 1) * 128],
                                 vg[0][:, et * 512:(et + 1) * 512],
                                 start=True, stop=False, skip_group_check=True)
                        return e.matmul(ps[4 + et2][:],
                                        lkTall[:, (2 * g + 1) * 128:(2 * g + 2) * 128],
                                        vg[1][:, et * 512:(et + 1) * 512],
                                        start=False, stop=True, skip_group_check=True)
                    p.on("tensor", mm_kv, waits=[("a0lkd", t_lkd), ("a0vev", t_vev),
                                                 ("a0kvup", p.cur("a0kvup"))],
                         inc="a0kvmm")
                t_kvm = p.cur("a0kvmm")

                def kv_add(v, half=half):
                    v.drain()
                    for et2 in range(2):
                        et = 2 * half + et2
                        v.tensor_add(kvf[:, et * 512:(et + 1) * 512],
                                     kvf[:, et * 512:(et + 1) * 512], ps[4 + et2][:])
                    return v.engine_nop()
                p.on("vector", kv_add, waits=[("a0kvmm", t_kvm)], inc="a0kvup")
            p.on("tensor", lambda e: e.nop(), inc="a0xc1")

    kv0recv = sb("a0kv0r", [128, 2048], F32)
    x_ready = x1(kvf[:], kv0recv[:], [("a0kvup", p.cur("a0kvup"))])
    kvbh = sb("a0kvbh", [128, 2048], BF16)
    kvbl = sb("a0kvbl", [128, 2048], BF16)
    kvtmp = sb("a0kvtmp", [128, 2048], F32)

    def kv_split(v):
        v.drain()
        v.tensor_copy(kvbh[:], kvf[:])
        v.drain()
        v.tensor_sub(kvtmp[:], kvf[:], kvbh[:])
        v.drain()
        return v.tensor_copy(kvbl[:], kvtmp[:])

    def kv_init(v):
        v.tensor_scalar(kvf[:], kv0recv[:], io["flagcol"][:, 0:1], None, ALU.mult)
        return kv_split(v)
    p.on("vector", kv_init, waits=[x_ready, ("a0kvup", p.cur("a0kvup"))], inc="a0kvup")

    with ExitStack() as ses:
        sb2 = lambda n, shp, dt: ses.enter_context(nc.sbuf_tensor(n, shp, dt))
        wg_sb = sb2("a0wg", [128, 8 * 2048], BF16)
        wout_sb = sb2("a0wo", [128, 16 * 1024], BF16)
        gateg = sb2("a0gate", [128, 16 * GR], BF16)
        og = sb2("a0og", [128, 16 * GR], BF16)
        qqh = sb2("a0qqh", [128, GR], BF16)
        qql = sb2("a0qql", [128, GR], BF16)
        qkh = sb2("a0qkh", [128, GR], BF16)
        qkl = sb2("a0qkl", [128, GR], BF16)
        lq = sb2("a0lq", [128, GR], BF16)
        qtmp = sb2("a0qtmp", [128, GR], F32)
        qtmp2 = sb2("a0qtmp2", [128, GR], F32)
        ta = sb2("a0ta", [128, GR + 128], F32)
        asq = sb2("a0asq", [128, GR + 128], BF16)
        bgc = emit_scale_cols(p, nc, ses, "a0bgc", io["bg"], 16, mg.identf,
                              id_ready, ps[6])
        t_sclg = p.cur("scl")
        sc_attn = sb2("a0scat", [128, 1], F32)
        bout_row = sb2("a0bor", [1, 1024], F32)
        p.on("gpsimd", lambda e: e.memset(sc_attn[:], ATT0_SCALE), inc="a0wl")
        wgv = wg_sb[:].rearrange("p (k e) -> p k e", k=8)
        wov = wout_sb[:].rearrange("p (k e) -> p k e", k=16)
        p.on("gpsimd", lambda e: e.dma_start(
            out=wgv, in_=io["Wg"].rearrange("(k p) e -> p k e", p=128)),
            inc="a0wl", inc_by=16)
        for q in range(2):
            p.on("gpsimd", lambda e, q=q: e.dma_start(
                out=wov[:, q * 8:(q + 1) * 8, :],
                in_=io["Wout"].rearrange("(k p) e -> p k e", p=128)[:, q * 8:(q + 1) * 8, :]),
                inc="a0wl", inc_by=16)
        p.on("sync", lambda e: e.dma_start(out=bout_row[:], in_=io["bout"].unsqueeze(0)),
             inc="a0wl", inc_by=16)
        t_wld2 = p.cur("a0wl")

        for g in range(NG):
            G0 = g * GR
            t_xl = emit_xg(G0, "a0xc2")
            for tj in range(2):
                p.on("sync", lambda e, G0=G0, tj=tj: e.dma_start(
                    out=vg[tj][:], in_=vdram[G0 + tj * 128:G0 + (tj + 1) * 128, :]),
                    waits=[("a0kvmm2", p.cur("a0kvmm2")), ("a0vomm", p.cur("a0vomm")),
                           ("a0vst", NG * 2 * 16)],
                    inc="a0vld", inc_by=16)
            t_vld = p.cur("a0vld")
            t_xq = emit_xq(t_wld2, t_xl, t_sclg)

            def affines(v):
                v.tensor_scalar(qtmp[:], xq[:], af9[:, 1:2], af9[:, 2:3], ALU.mult, ALU.add)
                v.tensor_scalar(qtmp2[:], xq[:], af9[:, 3:4], af9[:, 4:5], ALU.mult, ALU.add)
                v.tensor_scalar(lq[:], xq[:], af9[:, 5:6], af9[:, 6:7], ALU.mult, ALU.add)
                v.drain()
                v.tensor_copy(qqh[:], qtmp[:])
                v.tensor_copy(qkh[:], qtmp2[:])
                v.drain()
                v.tensor_sub(qtmp[:], qtmp[:], qqh[:])
                v.tensor_sub(qtmp2[:], qtmp2[:], qkh[:])
                v.drain()
                v.tensor_copy(qql[:], qtmp[:])
                return v.tensor_copy(qkl[:], qtmp2[:])
            p.on("vector", affines, waits=[("a0xqev", t_xq), ("a0amm", p.cur("a0amm")),
                                           ("a0vomm", p.cur("a0vomm"))], inc="a0affd")
            t_aff = p.cur("a0affd")
            for m in range(16):
                bank = ps[4 + (m % 2)]
                ev_w = [("a0gev", p.cur("a0gev"))]
                if m < 2:
                    ev_w += [("a0voev", p.cur("a0voev")), ("a0aev", p.cur("a0aev")),
                             ("a0kvcp", p.cur("a0kvcp"))]

                def mm_gate(e, m=m, bank=bank):
                    r = None
                    for k in range(8):
                        r = e.matmul(bank[:, 0:GR], wgv[:, k, m * 128:(m + 1) * 128],
                                     xg[:, k * GR:(k + 1) * GR],
                                     start=(k == 0), stop=(k == 7), skip_group_check=True)
                    return r
                p.on("tensor", mm_gate, waits=[("a0wl", t_wld2), ("a0xl", t_xl)] + ev_w,
                     inc="a0gmm")
                t_gm = p.cur("a0gmm")
                p.on("scalar", lambda e, m=m, bank=bank: e.activation(
                    gateg[:, m * GR:(m + 1) * GR], bank[:, 0:GR], AF.Silu,
                    bias=bgc[:, m:m + 1]),
                    waits=[("a0gmm", t_gm), ("scl", t_sclg)], inc="a0gev")
            t_gev = p.cur("a0gev")

            def mm_attn(e):
                e.matmul(ps[4][:, 0:GR], qkh[:, 0:128], qqh[:, 0:GR],
                         start=True, stop=False, skip_group_check=True)
                e.matmul(ps[4][:, 0:GR], qkh[:, 0:128], qql[:, 0:GR],
                         start=False, stop=False, skip_group_check=True)
                e.matmul(ps[4][:, 0:GR], qkl[:, 0:128], qqh[:, 0:GR],
                         start=False, stop=True, skip_group_check=True)
                e.matmul(ps[5][:, 0:128], qkh[:, 128:GR], qqh[:, 128:GR],
                         start=True, stop=False, skip_group_check=True)
                e.matmul(ps[5][:, 0:128], qkh[:, 128:GR], qql[:, 128:GR],
                         start=False, stop=False, skip_group_check=True)
                return e.matmul(ps[5][:, 0:128], qkl[:, 128:GR], qqh[:, 128:GR],
                                start=False, stop=True, skip_group_check=True)
            p.on("tensor", mm_attn, waits=[("a0affd", t_aff), ("a0gev", t_gev),
                                           ("a0asqd", p.cur("a0asqd"))], inc="a0amm")
            t_amm = p.cur("a0amm")
            p.on("scalar", lambda e: e.activation(ta[:, 0:GR], ps[4][:, 0:GR], AF.Relu,
                                                  scale=sc_attn[:, 0:1]),
                 waits=[("a0amm", t_amm), ("a0asq1", p.cur("a0asq1"))], inc="a0aev")
            p.on("scalar", lambda e: e.activation(ta[:, GR:GR + 128], ps[5][:, 0:128],
                                                  AF.Relu, scale=sc_attn[:, 0:1]),
                 waits=[("a0amm", t_amm)], inc="a0aev")
            t_aev = p.cur("a0aev")
            p.on("vector", lambda e: e.tensor_mul(asq[:], ta[:], ta[:]),
                 waits=[("a0aev", t_aev), ("a0vomm", p.cur("a0vomm"))], inc="a0asq1")
            t_asq1 = p.cur("a0asq1")

            def mask(gp):
                gp.affine_select(asq[:, 0:GR], asq[:, 0:GR], [[1, GR]], ALU.is_ge, 0.0,
                                 base=0, channel_multiplier=-1)
                gp.engine_nop()
                return gp.affine_select(asq[:, GR:GR + 128], asq[:, GR:GR + 128],
                                        [[1, 128]], ALU.is_ge, 0.0,
                                        base=0, channel_multiplier=-1)
            p.on("gpsimd", mask, waits=[("a0asq1", t_asq1)], inc="a0asqd")
            t_asqd = p.cur("a0asqd")
            for half in range(2):
                for et2 in range(2):
                    et = 2 * half + et2

                    def mm_kv(e, g=g, et=et, et2=et2):
                        e.matmul(ps[4 + et2][:], lkTall[:, 2 * g * 128:(2 * g + 1) * 128],
                                 vg[0][:, et * 512:(et + 1) * 512],
                                 start=True, stop=False, skip_group_check=True)
                        return e.matmul(ps[4 + et2][:],
                                        lkTall[:, (2 * g + 1) * 128:(2 * g + 2) * 128],
                                        vg[1][:, et * 512:(et + 1) * 512],
                                        start=False, stop=True, skip_group_check=True)
                    p.on("tensor", mm_kv, waits=[("a0vld", t_vld), ("a0aev", t_aev),
                                                 ("a0gev", t_gev),
                                                 ("a0kvcp", p.cur("a0kvcp")),
                                                 ("a0kvup", p.cur("a0kvup"))],
                         inc="a0kvmm2")
                t_kvm = p.cur("a0kvmm2")
                p.on("vector", lambda e, half=half: (
                    e.drain(),
                    e.tensor_copy(kvtmp[:, 2 * half * 512:(2 * half + 1) * 512], ps[4][:]),
                    e.tensor_copy(kvtmp[:, (2 * half + 1) * 512:(2 * half + 2) * 512],
                                  ps[5][:]))[-1],
                    waits=[("a0kvmm2", t_kvm), ("a0kvup", p.cur("a0kvup"))],
                    inc="a0kvcp")
            t_kvcp = p.cur("a0kvcp")
            for ec in range(16):
                bank = ps[0 + (ec % 2)]
                ev_w = [("a0voev", p.cur("a0voev"))]

                def mm_vo(e, ec=ec, bank=bank):
                    e.matmul(bank[:, 0:GR], vg[0][:, ec * 128:(ec + 1) * 128],
                             asq[:, 0:GR], start=True, stop=False, skip_group_check=True)
                    e.matmul(bank[:, 128:GR], vg[1][:, ec * 128:(ec + 1) * 128],
                             asq[:, GR:GR + 128], start=False, stop=False,
                             skip_group_check=True)
                    e.matmul(bank[:, 0:GR], kvbh[:, ec * 128:(ec + 1) * 128],
                             lq[:, 0:GR], start=False, stop=False, skip_group_check=True)
                    return e.matmul(bank[:, 0:GR], kvbl[:, ec * 128:(ec + 1) * 128],
                                    lq[:, 0:GR], start=False, stop=True,
                                    skip_group_check=True)
                p.on("tensor", mm_vo, waits=[("a0asqd", t_asqd), ("a0kvup", p.cur("a0kvup")),
                                             ("a0vld", t_vld)] + ev_w, inc="a0vomm")
                t_vom = p.cur("a0vomm")
                p.on("vector", lambda e, ec=ec, bank=bank: e.tensor_mul(
                    og[:, ec * GR:(ec + 1) * GR], bank[:, 0:GR],
                    gateg[:, ec * GR:(ec + 1) * GR]),
                    waits=[("a0vomm", t_vom), ("a0gev", t_gev),
                           ("a0womm", p.cur("a0womm"))], inc="a0voev")
            t_voev = p.cur("a0voev")

            def kv_update(v):
                v.drain()
                for et in range(4):
                    v.tensor_add(kvf[:, et * 512:(et + 1) * 512],
                                 kvf[:, et * 512:(et + 1) * 512],
                                 kvtmp[:, et * 512:(et + 1) * 512])
                return kv_split(v)
            p.on("vector", kv_update, waits=[("a0kvcp", t_kvcp),
                                             ("a0vomm", p.cur("a0vomm"))], inc="a0kvup")
            for tj in range(2):
                rj = G0 + tj * 128
                ywaits = [("a0voev", t_voev), ("a0wl", t_wld2), ("s3", p.cur("s3"))]

                def mm_y(e, tj=tj):
                    r = None
                    for q in range(2):
                        for ec in range(16):
                            e.matmul(ps[6 + q][:],
                                     og[:, ec * GR + tj * 128:ec * GR + (tj + 1) * 128],
                                     wov[:, ec, q * 512:(q + 1) * 512],
                                     start=(ec == 0), stop=False, skip_group_check=True)
                        r = e.matmul(ps[6 + q][:], mg.ones1[:],
                                     bout_row[0:1, q * 512:(q + 1) * 512],
                                     start=False, stop=True, skip_group_check=True)
                    return r
                p.on("tensor", mm_y, waits=ywaits, inc="a0womm")
                t_w = p.cur("a0womm")
                p.on("vector", lambda e: (e.tensor_copy(mg.s3t.ysb[:, 0:512], ps[6][:]),
                                          e.tensor_copy(mg.s3t.ysb[:, 512:1024], ps[7][:]))[-1],
                     waits=[("a0womm", t_w)], inc="yev")
                t_yev = p.cur("yev")
                emit_s3(p, mg.lns, mg.s3t, io["resin"], io["resout"], io["lnxout"],
                        rj, [("yev", t_yev)])
            p.on("tensor", lambda e: e.nop(), inc="a0xc2")
    oes.close()


def m_ffn0(mg, io, id_ready, TT_=512, dbg_flags=()):

    nc, es, p, ps = mg.nc, mg.es, mg.p, mg.ps
    lnx = io["lnx"]
    with ExitStack() as ses:
        sb = lambda n, shp, dt: ses.enter_context(nc.sbuf_tensor(n, shp, dt))
        xfm = sb("f0xfm", [128, 8 * TT_], BF16)
        hbuf = sb("f0h", [128, 32 * TT_], BF16)
        g1buf = sb("f0g1", [128, 32 * TT_], BF16)
        w12 = [sb(f"f0w12_{b}", [128, 2 * 8 * 128], BF16) for b in range(3)]
        hstr = [sb(f"f0hs_{b}", [128, 16 * 128], BF16) for b in range(3)]
        w3sb = sb("f0w3", [128, 32 * 1024], BF16)
        w3bc = sb("f0w3bc", [128, 1024], F32)
        w3row = sb("f0w3r", [1, 1024], F32)
        t1sil = sb("f0t1", [128, TT_], F32)
        scl = {}
        for nm, vec in [("w1s", io["w1s"]), ("w2s", io["w2s"]),
                        ("h1s", io["h1s"]), ("h2s", io["h2s"])]:
            scl[nm] = emit_scale_cols(p, nc, ses, "f0" + nm, vec, 32, mg.identf,
                                      id_ready, ps[6])
        t_scl = p.cur("scl")
        if "dbg_scl" in dbg_flags:
            for nm in ["w1s", "w2s", "h1s", "h2s"]:
                od = nc.dram_tensor("dbgscl_" + nm, [128, 32], F32,
                                    kind="ExternalOutput").ap()
                p.on("sync", lambda e, od=od, nm=nm: e.dma_start(out=od, in_=scl[nm][:]),
                     waits=[("scl", t_scl)], inc="st", inc_by=16)
        w3r = io["w3T"].rearrange("(k p) d -> p k d", p=128)
        w3v = w3sb[:].rearrange("p (k d) -> p k d", d=1024)
        for q in range(4):
            p.on("gpsimd", lambda e, q=q: e.dma_start(
                out=w3v[:, q * 8:(q + 1) * 8, :], in_=w3r[:, q * 8:(q + 1) * 8, :]),
                inc="f0w3ld", inc_by=16)
        p.on("sync", lambda e: e.dma_start(out=w3row[:], in_=io["w3s"].unsqueeze(0)),
             inc="f0w3ld", inc_by=16)
        t_w3 = p.cur("f0w3ld")
        for q in range(2):
            p.on("tensor", lambda e, q=q: e.matmul(
                ps[6 + q][:], mg.ones1[:], w3row[0:1, q * 512:(q + 1) * 512],
                start=True, stop=True, skip_group_check=True),
                waits=[("f0w3ld", t_w3), ("scl", t_scl), id_ready,
                       ("yev", p.cur("yev"))], inc="f0bc")
        t_bc = p.cur("f0bc")
        p.on("vector", lambda e: (e.tensor_copy(w3bc[:, 0:512], ps[6][:]),
                                  e.tensor_copy(w3bc[:, 512:1024], ps[7][:]))[-1],
             waits=[("f0bc", t_bc)], inc="f0bc")
        t_bc2 = p.cur("f0bc")
        if "dbg_scl" in dbg_flags:
            od = nc.dram_tensor("dbgscl_w3bc", [128, 1024], F32,
                                kind="ExternalOutput").ap()
            p.on("sync", lambda e, od=od: e.dma_start(out=od, in_=w3bc[:]),
                 waits=[("f0bc", t_bc2)], inc="st", inc_by=16)

        ntile = TH // TT_
        n128 = TT_ // 128
        for i in range(ntile):
            r0 = i * TT_
            xw = [("f0h1mm", p.cur("f0h1mm"))] if i else []
            for c in range(NCH):
                p.on("sync", lambda e, c=c, r0=r0: e.dma_start(
                    out=xfm[:, c * TT_:(c + 1) * TT_],
                    in_=lnx[r0:r0 + TT_, c * 128:(c + 1) * 128], transpose=True),
                    waits=xw, inc="f0xl", inc_by=16)
            t_xl = p.cur("f0xl")
            for m in range(32):
                b = m % 3
                wv = w12[b][:].rearrange("p (w k c) -> p w k c", w=2, c=128)
                wl_w = [("f0h1mm", max(0, p.cur("f0h1mm") - 1))]
                p.on("gpsimd", lambda e, m=m, wv=wv: e.dma_start(
                    out=wv[:, 0],
                    in_=io["w1T"].rearrange("(k p) m -> p k m", p=128)[:, :, m * 128:(m + 1) * 128]),
                    waits=wl_w, inc="f0wl", inc_by=16)
                p.on("gpsimd", lambda e, m=m, wv=wv: e.dma_start(
                    out=wv[:, 1],
                    in_=io["w2T"].rearrange("(k p) m -> p k m", p=128)[:, :, m * 128:(m + 1) * 128]),
                    inc="f0wl", inc_by=16)
                t_wl = p.cur("f0wl")
                pa, pb = ps[2 * b], ps[2 * b + 1]
                ev_w = [("f0h1ev", p.cur("f0h1ev"))]
                if m < 2:
                    ev_w.append(("s3", p.cur("s3")))

                def mm_h1(e, pa=pa, pb=pb, wv=wv):
                    r = None
                    for k in range(8):
                        e.matmul(pa[:, 0:TT_], wv[:, 0, k], xfm[:, k * TT_:(k + 1) * TT_],
                                 start=(k == 0), stop=(k == 7), skip_group_check=True)
                        r = e.matmul(pb[:, 0:TT_], wv[:, 1, k], xfm[:, k * TT_:(k + 1) * TT_],
                                     start=(k == 0), stop=(k == 7), skip_group_check=True)
                    return r
                p.on("tensor", mm_h1, waits=[("f0wl", t_wl), ("f0xl", t_xl)] + ev_w,
                     inc="f0h1mm")
                t_mm = p.cur("f0h1mm")
                p.on("scalar", lambda e, m=m, pa=pa: e.activation(
                    t1sil[:], pa[:, 0:TT_], AF.Silu, scale=scl["w1s"][:, m:m + 1]),
                    waits=[("f0h1mm", t_mm), ("scl", t_scl),
                           ("f0h1ev", p.cur("f0h1ev"))], inc="f0h1sil")
                t_sil = p.cur("f0h1sil")
                p.on("vector", lambda e, m=m, pb=pb: e.scalar_tensor_tensor(
                    hbuf[:, m * TT_:(m + 1) * TT_], pb[:, 0:TT_], scl["w2s"][:, m:m + 1],
                    t1sil[:], ALU.mult, ALU.mult),
                    waits=[("f0h1sil", t_sil), ("f0ymm", p.cur("f0ymm"))], inc="f0h1ev")
            t_h1 = p.cur("f0h1ev")
            for phase in range(2):
                wdram = io["h1T"] if phase == 0 else io["h2T"]
                sname = "h1s" if phase == 0 else "h2s"
                src_b = hbuf if phase == 0 else g1buf
                dst_b = g1buf if phase == 0 else hbuf
                mmsem, evsem = f"f0g{phase}mm", f"f0g{phase}ev"
                for m in range(32):
                    bank = ps[4 + (m % 2)]
                    for half in range(2):
                        hb = (2 * m + half) % 3
                        wl_w = [(mmsem, max(0, p.cur(mmsem) - 1))]
                        if phase == 1 and m < 2:
                            wl_w.append(("f0g0mm", p.cur("f0g0mm")))
                        p.on("gpsimd", lambda e, m=m, hb=hb, half=half, wdram=wdram: e.dma_start(
                            out=hstr[hb][:].rearrange("p (k c) -> p k c", c=128),
                            in_=wdram.rearrange("(k p) m -> p k m", p=128)[
                                :, half * 16:(half + 1) * 16, m * 128:(m + 1) * 128]),
                            waits=wl_w, inc="f0wl", inc_by=16)
                        t_wl = p.cur("f0wl")
                        ev_w = [(evsem, p.cur(evsem))] if half == 0 else []
                        if m == 0 and half == 0:
                            if phase == 0:
                                ev_w += [("f0h1ev", t_h1), ("f0g1ev", p.cur("f0g1ev")),
                                         ("f0g1mm", p.cur("f0g1mm"))]
                            else:
                                ev_w += [("f0g0mm", p.cur("f0g0mm")), ("f0g0ev", p.cur("f0g0ev")),
                                         ("f0ymm", p.cur("f0ymm"))]

                        def mm_gh(e, hb=hb, half=half, bank=bank, src_b=src_b):
                            r = None
                            for k2 in range(16):
                                k = half * 16 + k2
                                r = e.matmul(bank[:, 0:TT_],
                                             hstr[hb][:, k2 * 128:(k2 + 1) * 128],
                                             src_b[:, k * TT_:(k + 1) * TT_],
                                             start=(k == 0), stop=(k == 31),
                                             skip_group_check=True)
                            return r
                        p.on("tensor", mm_gh, waits=[("f0wl", t_wl)] + ev_w, inc=mmsem)
                    t_mg = p.cur(mmsem)
                    p.on("scalar", lambda e, m=m, bank=bank, sname=sname, dst_b=dst_b: e.activation(
                        dst_b[:, m * TT_:(m + 1) * TT_], bank[:, 0:TT_], AF.Gelu,
                        scale=scl[sname][:, m:m + 1]),
                        waits=[(mmsem, t_mg)], inc=evsem)
            t_g2 = p.cur("f0g1ev")
            for j in range(n128):
                rj = r0 + j * 128
                py0, py1 = ps[2 * (j % 2)], ps[2 * (j % 2) + 1]
                ywaits = [("f0g1ev", t_g2), ("f0w3ld", t_w3), ("s3", p.cur("s3")),
                          ("f0h1sil", p.cur("f0h1sil"))]

                def mm_y(e, j=j, py0=py0, py1=py1):
                    r = None
                    for k in range(32):
                        lh = hbuf[:, k * TT_ + j * 128: k * TT_ + (j + 1) * 128]
                        e.matmul(py0[:], lh, w3v[:, k, 0:512],
                                 start=(k == 0), stop=(k == 31), skip_group_check=True)
                        r = e.matmul(py1[:], lh, w3v[:, k, 512:1024],
                                     start=(k == 0), stop=(k == 31), skip_group_check=True)
                    return r
                p.on("tensor", mm_y, waits=ywaits, inc="f0ymm")
                t_ymm = p.cur("f0ymm")
                p.on("vector", lambda e, py0=py0, py1=py1: (
                    e.tensor_mul(mg.s3t.ysb[:, 0:512], py0[:], w3bc[:, 0:512]),
                    e.tensor_mul(mg.s3t.ysb[:, 512:1024], py1[:], w3bc[:, 512:1024]))[-1],
                    waits=[("f0ymm", t_ymm), ("f0bc", t_bc2)], inc="yev")
                t_yev = p.cur("yev")
                emit_s3(p, mg.lns, mg.s3t, io["resin"], io["resout"], io["lnxout"],
                        rj, [("yev", t_yev)])


def m_ffn1(mg, io, id_ready, PT=1024):
    nc, es, p, ps = mg.nc, mg.es, mg.p, mg.ps
    with ExitStack() as ses:
        sb = lambda n, shp, dt: ses.enter_context(nc.sbuf_tensor(n, shp, dt))
        win_sb = sb("f1win", [128, 8 * 64], BF16)
        wout_sb = sb("f1wo", [128, 1024], BF16)
        xfm = sb("f1xfm", [128, 8 * PT], BF16)
        W2T = PT // 2
        s_sb = sb("f1s", [128, W2T], F32)
        tbuf = sb("f1t", [128, W2T], F32)
        gacc = sb("f1g", [128, W2T], F32)
        gaccb = sb("f1gb", [128, W2T], BF16)
        w1c = sb("f1w1c", [128, 16], F32)
        b1c = sb("f1b1c", [128, 16], F32)
        w2c = sb("f1w2c", [128, 16], F32)
        binc = sb("f1binc", [128, 1], F32)
        b2col = sb("f1b2c", [64, 1], F32)
        b2colb = sb("f1b2cb", [64, 1], BF16)
        brow = sb("f1brow", [1, 1024], F32)
        boutrow = sb("f1boutr", [1, 1024], F32)
        rows1 = sb("f1rows", [1, 64], F32)
        p.on("gpsimd", lambda e: e.dma_start(
            out=win_sb[:].rearrange("p (k e) -> p k e", k=8),
            in_=io["Win"].rearrange("(k p) e -> p k e", p=128)), inc="f1wl", inc_by=16)
        p.on("gpsimd", lambda e: e.dma_start(out=wout_sb[0:64, :], in_=io["Wout"]),
             inc="f1wl", inc_by=16)
        p.on("gpsimd", lambda e: e.dma_start(out=wout_sb[64:128, :], in_=io["Wout"]),
             inc="f1wl", inc_by=16)
        for dst, src_ in [(w1c, io["W1"]), (b1c, io["b1"]), (w2c, io["W2"])]:
            p.on("sync", lambda e, dst=dst, src_=src_: e.dma_start(out=dst[0:64, :], in_=src_),
                 inc="f1wl", inc_by=16)
            p.on("sync", lambda e, dst=dst, src_=src_: e.dma_start(out=dst[64:128, :], in_=src_),
                 inc="f1wl", inc_by=16)
        p.on("sync", lambda e: e.dma_start(out=rows1[:], in_=io["bin_"].unsqueeze(0)),
             inc="f1wl", inc_by=16)
        p.on("sync", lambda e: e.dma_start(out=boutrow[:], in_=io["bout"].unsqueeze(0)),
             inc="f1wl", inc_by=16)
        t_wld = p.cur("f1wl")
        tt = p.on("tensor", lambda e: e.matmul(ps[6][0:64, 0:1], rows1[:],
                                               mg.identf[0:1, 0:1], start=True,
                                               stop=True, skip_group_check=True),
                  waits=[("f1wl", t_wld), id_ready, ("yev", p.cur("yev"))], inc="f1su")
        p.on("vector", lambda e: (e.tensor_copy(binc[0:64, :], ps[6][0:64, 0:1]),
                                  e.tensor_copy(binc[64:128, :], ps[6][0:64, 0:1]))[-1],
             waits=[("f1su", tt)], inc="f1su")
        t_b = p.on("sync", lambda e: e.dma_start(out=rows1[0:1, 0:64], in_=io["b2"].unsqueeze(0)),
                   waits=[("f1su", p.cur("f1su"))], inc="f1wl", inc_by=16)
        tt2 = p.on("tensor", lambda e: e.matmul(ps[6][0:64, 0:1], rows1[0:1, 0:64],
                                                mg.identf[0:1, 0:1], start=True,
                                                stop=True, skip_group_check=True),
                   waits=[("f1wl", t_b), ("f1su", p.cur("f1su"))], inc="f1su")
        p.on("vector", lambda e: (e.tensor_copy(b2col[:], ps[6][0:64, 0:1]),
                                  e.drain(),
                                  e.tensor_copy(b2colb[:], b2col[:]))[-1],
             waits=[("f1su", tt2)], inc="f1su")
        t_su = p.cur("f1su")
        for q in range(2):
            p.on("tensor", lambda e, q=q: e.matmul(
                ps[6 + q][0:1, :], b2colb[:], wout_sb[0:64, q * 512:(q + 1) * 512],
                start=True, stop=True, skip_group_check=True),
                waits=[("f1su", t_su), ("f1wl", t_wld)], inc="f1su")
        t_su2 = p.cur("f1su")
        p.on("vector", lambda e: (e.tensor_add(brow[0:1, 0:512], ps[6][0:1, :],
                                               boutrow[0:1, 0:512]),
                                  e.tensor_add(brow[0:1, 512:1024], ps[7][0:1, :],
                                               boutrow[0:1, 512:1024]))[-1],
             waits=[("f1su", t_su2)], inc="f1su")
        t_setup = p.cur("f1su")
        winv = win_sb[:].rearrange("p (k e) -> p k e", k=8)
        for ip in range(TH // PT):
            r0 = ip * PT
            xw = [("f1smm", p.cur("f1smm"))] if ip else []
            for c in range(NCH):
                p.on("sync", lambda e, c=c, r0=r0: e.dma_start(
                    out=xfm[:, c * PT:(c + 1) * PT],
                    in_=io["lnx"][r0:r0 + PT, c * 128:(c + 1) * 128], transpose=True),
                    waits=xw, inc="f1xl", inc_by=16)
            t_xl = p.cur("f1xl")

            def mm_s(e):
                r = None
                for half in range(2):
                    o = ps[0][half * 64:(half + 1) * 64, 0:W2T]
                    for k in range(NCH):
                        r = e.matmul(o, winv[:, k, :],
                                     xfm[:, k * PT + half * W2T:k * PT + (half + 1) * W2T],
                                     start=(k == 0), stop=(k == 7), skip_group_check=True)
                return r
            p.on("tensor", mm_s, waits=[("f1wl", t_wld), ("f1xl", t_xl),
                                        ("f1gd", p.cur("f1gd"))], inc="f1smm")
            t_smm = p.cur("f1smm")
            p.on("vector", lambda e: e.tensor_scalar(s_sb[:], ps[0][:, 0:W2T],
                                                     binc[:, 0:1], None, ALU.add),
                 waits=[("f1smm", t_smm), ("f1su", t_setup)], inc="f1sev")
            t_sev = p.cur("f1sev")

            def g_chain(v):
                v.memset(gacc[:], 0.0)
                for i in range(16):
                    v.drain()
                    v.tensor_scalar(tbuf[:], s_sb[:], w1c[:, i:i + 1], b1c[:, i:i + 1],
                                    ALU.mult, ALU.add)
                    v.drain()
                    v.tensor_scalar_max(tbuf[:], tbuf[:], 0.0)
                    v.drain()
                    v.scalar_tensor_tensor(gacc[:], tbuf[:], w2c[:, i:i + 1], gacc[:],
                                           ALU.mult, ALU.add)
                v.drain()
                return v.tensor_copy(gaccb[:], gacc[:])
            p.on("vector", g_chain, waits=[("f1sev", t_sev), ("f1omm", p.cur("f1omm"))],
                 inc="f1gd")
            t_gd = p.cur("f1gd")
            for j in range(PT // 128):
                half = j // (W2T // 128)
                jj = j % (W2T // 128)
                rj = r0 + j * 128
                ywaits = [("f1gd", t_gd), ("s3", p.cur("s3")), ("f1su", t_setup)]

                def mm_o(e, half=half, jj=jj):
                    r = None
                    for q in range(2):
                        lh = gaccb[half * 64:(half + 1) * 64, jj * 128:(jj + 1) * 128]
                        e.matmul(ps[6 + q][:], lh,
                                 wout_sb[half * 64:(half + 1) * 64, q * 512:(q + 1) * 512],
                                 start=True, stop=False, skip_group_check=True)
                        r = e.matmul(ps[6 + q][:], mg.ones1[:],
                                     brow[0:1, q * 512:(q + 1) * 512],
                                     start=False, stop=True, skip_group_check=True)
                    return r
                p.on("tensor", mm_o, waits=ywaits, inc="f1omm")
                t_o = p.cur("f1omm")
                p.on("vector", lambda e: (e.tensor_copy(mg.s3t.ysb[:, 0:512], ps[6][:]),
                                          e.tensor_copy(mg.s3t.ysb[:, 512:1024], ps[7][:]))[-1],
                     waits=[("f1omm", t_o)], inc="yev")
                t_yev = p.cur("yev")
                emit_s3(p, mg.lns, mg.s3t, io["resin"], io["out"], None, rj,
                        [("yev", t_yev)], final=True)


def m_att1(mg, io, id_ready, x3, W=256):
    nc, es, p, ps = mg.nc, mg.es, mg.p, mg.ps
    nwin = TH // W
    WX = W + 128
    AW = W + 1
    oes = ExitStack()
    sb = lambda n, shp, dt: oes.enter_context(nc.sbuf_tensor(n, shp, dt))
    xw = sb("a1xw", [128, 8 * WX], BF16)
    mixb = sb("a1mix", [128, 8 * W], BF16)
    ek = sb("a1ek", [128, 8 * W], F32)
    vv = sb("a1vv", [128, 8 * W], F32)
    abuf = sb("a1ab", [128, 8 * AW], F32)
    bbuf = sb("a1bb", [128, 8 * AW], F32)
    ewt = sb("a1ewt", [128, 8 * W], F32)
    abcar = sb("a1car", [128, 16], F32)
    abrecv = sb("a1rcv", [128, 16], F32)
    wdec = sb("a1wdec", [128, 8], F32)
    ewc = sb("a1ewc", [128, 8], F32)
    euc = sb("a1euc", [128, 8], F32)
    omk = sb("a1omk", [128, 8], F32)
    omv = sb("a1omv", [128, 8], F32)
    omr = sb("a1omr", [128, 8], F32)
    iot = sb("a1iot", [128, W], F32)
    iow = sb("a1iow", [128, nwin], F32)
    pwtab = sb("a1pw", [128, 8 * W], F32)
    cwtab = sb("a1cw", [128, 8 * nwin], F32)
    ptmp = sb("a1ptmp", [128, W], F32)
    saw = sb("a1saw", [128, 16], F32)
    zpad = sb("a1zpad", [128, D], BF16)

    tdc = emit_scale_cols(p, nc, oes, "a1tdc", io["tvec5"][0], 8, mg.identf, id_ready, ps[6])
    tfc = emit_scale_cols(p, nc, oes, "a1tfc", io["tvec5"][1], 8, mg.identf, id_ready, ps[6])
    tmkc = emit_scale_cols(p, nc, oes, "a1tmk", io["tvec5"][2], 8, mg.identf, id_ready, ps[6])
    tmvc = emit_scale_cols(p, nc, oes, "a1tmv", io["tvec5"][3], 8, mg.identf, id_ready, ps[6])
    tmrc = emit_scale_cols(p, nc, oes, "a1tmr", io["tvec5"][4], 8, mg.identf, id_ready, ps[6])
    t_scl = p.cur("scl")

    p.on("gpsimd", lambda e: (e.memset(ewt[:], 1.0), e.memset(zpad[:], 0.0),
                              e.engine_nop(),
                              e.iota(iot[:], [[1, W]], base=0, channel_multiplier=0,
                                     allow_small_or_imprecise_dtypes=True),
                              e.iota(iow[:], [[W, nwin]], base=0, channel_multiplier=0,
                                     allow_small_or_imprecise_dtypes=True))[-1],
         inc="a1cst")
    t_c0 = p.cur("a1cst")
    p.on("scalar", lambda e: e.activation(wdec[:], tdc[:], AF.Exp),
         waits=[("scl", t_scl)], inc="a1cst")
    t_c1 = p.cur("a1cst")
    p.on("scalar", lambda e: e.activation(ewc[:], wdec[:], AF.Exp, scale=-1.0),
         waits=[("a1cst", t_c1)], inc="a1cst")
    p.on("scalar", lambda e: e.activation(euc[:], tfc[:], AF.Exp),
         waits=[("scl", t_scl)], inc="a1cst")
    t_c2 = p.cur("a1cst")

    def dve_consts(v):
        v.tensor_scalar(omk[:], tmkc[:], -1.0, 1.0, ALU.mult, ALU.add)
        v.tensor_scalar(omv[:], tmvc[:], -1.0, 1.0, ALU.mult, ALU.add)
        v.tensor_scalar(omr[:], tmrc[:], -1.0, 1.0, ALU.mult, ALU.add)
        v.memset(abcar[:], 0.0)
        v.drain()
        for c in range(NCH):
            v.tensor_scalar(ewt[:, c * W:(c + 1) * W], ewt[:, c * W:(c + 1) * W],
                            ewc[:, c:c + 1], None, ALU.mult)
        v.drain()
        return v.engine_nop()
    p.on("vector", dve_consts, waits=[("a1cst", t_c2), ("scl", t_scl)], inc="a1cst")
    t_cst = p.cur("a1cst")
    for c in range(NCH):
        t_p1 = p.on("vector", lambda e, c=c: e.tensor_scalar(
            ptmp[:], iot[:], wdec[:, c:c + 1], None, ALU.mult),
            waits=[("a1cst", t_cst), ("a1pwd", p.cur("a1pwd"))], inc="a1pwd")
        p.on("scalar", lambda e, c=c: e.activation(
            pwtab[:, c * W:(c + 1) * W], ptmp[:], AF.Exp, scale=-1.0),
            waits=[("a1pwd", t_p1)], inc="a1pwd")
        t_p2 = p.on("vector", lambda e, c=c: e.tensor_scalar(
            ptmp[:, 0:nwin], iow[:], wdec[:, c:c + 1], None, ALU.mult),
            waits=[("a1pwd", p.cur("a1pwd"))], inc="a1pwd")
        p.on("scalar", lambda e, c=c: e.activation(
            cwtab[:, c * nwin:(c + 1) * nwin], ptmp[:, 0:nwin], AF.Exp, scale=-1.0),
            waits=[("a1pwd", t_p2)], inc="a1pwd")
    t_pw = p.cur("a1pwd")

    p.on("sync", lambda e: e.dma_start(out=io["lnx2p"][TH + 1:TH + 129, :], in_=zpad[:]),
         waits=[("a1cst", t_c0)], inc="a1zp", inc_by=16)
    t_zp = p.cur("a1zp")

    wk_sb = sb("a1wk", [128, 8 * D], BF16)
    wv_sb = sb("a1wv", [128, 8 * D], BF16)
    for nm, dram, dst in [("wk", io["wkT"], wk_sb), ("wv", io["wvT"], wv_sb)]:
        p.on("gpsimd", lambda e, dram=dram, dst=dst: e.dma_start(
            out=dst[:].rearrange("p (k e) -> p k e", k=8),
            in_=dram.rearrange("(k p) e -> p k e", p=128)), inc="a1wl", inc_by=16)
    t_wld = p.cur("a1wl")
    wkv = wk_sb[:].rearrange("p (k e) -> p k e", k=8)
    wvv = wv_sb[:].rearrange("p (k e) -> p k e", k=8)

    def emit_xw(w, gate):
        R0 = w * W
        xwaits = [(gate, p.cur(gate))]
        if w == nwin - 1:
            xwaits.append(("a1zp", t_zp))
        for c in range(NCH):
            p.on("sync", lambda e, c=c, R0=R0: e.dma_start(
                out=xw[:, c * WX:(c + 1) * WX],
                in_=io["lnx2p"][R0:R0 + WX, c * 128:(c + 1) * 128], transpose=True),
                waits=xwaits, inc="a1xl", inc_by=16)
        return p.cur("a1xl")

    def emit_mix(t_xl, mixc, omc, extra=()):
        def mixes(v, mixc=mixc, omc=omc):
            for c in range(NCH):
                v.tensor_scalar(mixb[:, c * W:(c + 1) * W],
                                xw[:, c * WX + 1:c * WX + 1 + W],
                                mixc[:, c:c + 1], None, ALU.mult)
            v.drain()
            r = None
            for c in range(NCH):
                r = v.scalar_tensor_tensor(mixb[:, c * W:(c + 1) * W],
                                           xw[:, c * WX:c * WX + W], omc[:, c:c + 1],
                                           mixb[:, c * W:(c + 1) * W], ALU.mult, ALU.add)
            return r
        p.on("vector", mixes, waits=[("a1xl", t_xl), ("a1cst", t_cst),
                                     ("a1pmm", p.cur("a1pmm"))] + list(extra), inc="a1mixd")
        return p.cur("a1mixd")

    def emit_proj(t_mix, t_wl, wmat, evac, evac_waits=()):
        for m in range(NCH):
            bank = ps[m % 2]
            ev_w = [("a1pev", p.cur("a1pev"))]

            def mm_p(e, m=m, bank=bank, wmat=wmat):
                r = None
                for k in range(NCH):
                    r = e.matmul(bank[:, 0:W], wmat[:, k, m * 128:(m + 1) * 128],
                                 mixb[:, k * W:(k + 1) * W],
                                 start=(k == 0), stop=(k == 7), skip_group_check=True)
                return r
            p.on("tensor", mm_p, waits=[("a1wl", t_wl), ("a1mixd", t_mix)] + ev_w,
                 inc="a1pmm")
            t_mm = p.cur("a1pmm")
            p.on("scalar", lambda e, m=m, bank=bank: evac(e, m, bank),
                 waits=[("a1pmm", t_mm)] + list(evac_waits), inc="a1pev")
        return p.cur("a1pev")

    stash_t = []
    for w in range(nwin):
        t_xl = emit_xw(w, "a1mixd")
        t_mix = emit_mix(t_xl, tmkc, omk)
        evw = [("a1dend", p.cur("a1dend"))]
        t_pev = emit_proj(t_mix, t_wld, wkv,
                          lambda e, m, bank: e.activation(
                              ek[:, m * W:(m + 1) * W], bank[:, 0:W], AF.Exp),
                          evac_waits=evw + [("a1st", p.cur("a1st"))])
        t_mix = emit_mix(t_xl, tmvc, omv, extra=[("a1pev", t_pev)])
        t_pev = emit_proj(t_mix, t_wld, wvv,
                          lambda e, m, bank: e.copy(
                              vv[:, m * W:(m + 1) * W], bank[:, 0:W]),
                          evac_waits=evw + [("a1st", p.cur("a1st"))])

        def scan_block(v):
            for c in range(NCH):
                v.tensor_mul(vv[:, c * W:(c + 1) * W], vv[:, c * W:(c + 1) * W],
                             ek[:, c * W:(c + 1) * W])
            v.drain()
            for c in range(NCH):
                v.tensor_copy(abuf[:, c * AW:c * AW + 1], abcar[:, c:c + 1])
                v.tensor_copy(bbuf[:, c * AW:c * AW + 1], abcar[:, 8 + c:9 + c])
            v.drain()
            for c in range(NCH):
                v.tensor_tensor_scan(abuf[:, c * AW + 1:c * AW + 1 + W],
                                     ewt[:, c * W:(c + 1) * W],
                                     vv[:, c * W:(c + 1) * W],
                                     abcar[:, c:c + 1], ALU.mult, ALU.add)
                v.tensor_tensor_scan(bbuf[:, c * AW + 1:c * AW + 1 + W],
                                     ewt[:, c * W:(c + 1) * W],
                                     ek[:, c * W:(c + 1) * W],
                                     abcar[:, 8 + c:9 + c], ALU.mult, ALU.add)
            v.drain()
            for c in range(NCH):
                v.tensor_copy(abcar[:, c:c + 1], abuf[:, c * AW + W:c * AW + W + 1])
                v.tensor_copy(abcar[:, 8 + c:9 + c], bbuf[:, c * AW + W:c * AW + W + 1])
            v.drain()
            r = None
            for c in range(NCH):
                v.scalar_tensor_tensor(vv[:, c * W:(c + 1) * W],
                                       vv[:, c * W:(c + 1) * W], euc[:, c:c + 1],
                                       abuf[:, c * AW:c * AW + W], ALU.mult, ALU.add)
                r = v.scalar_tensor_tensor(ek[:, c * W:(c + 1) * W],
                                           ek[:, c * W:(c + 1) * W], euc[:, c:c + 1],
                                           bbuf[:, c * AW:c * AW + W], ALU.mult, ALU.add)
            return r
        p.on("vector", scan_block, waits=[("a1pev", t_pev), ("a1pwd", t_pw)],
             inc="a1dend")
        t_den = p.cur("a1dend")
        for c in range(NCH):
            p.on("sync", lambda e, w=w, c=c: e.dma_start(
                out=io["numd"][:, c * TH + w * W:c * TH + (w + 1) * W],
                in_=vv[:, c * W:(c + 1) * W]),
                waits=[("a1dend", t_den)], inc="a1st", inc_by=16)
            p.on("sync", lambda e, w=w, c=c: e.dma_start(
                out=io["dend"][:, c * TH + w * W:c * TH + (w + 1) * W],
                in_=ek[:, c * W:(c + 1) * W]),
                waits=[("a1dend", t_den)], inc="a1st", inc_by=16)
        stash_t.append(p.cur("a1st"))

    x_ready = x3(abcar[:], abrecv[:], [("a1dend", p.cur("a1dend"))])
    p.on("vector", lambda e: e.tensor_scalar(abrecv[:], abrecv[:],
                                             io["flagcol"][:, 0:1], None, ALU.mult),
         waits=[x_ready], inc="a1rdy")
    t_rdy = p.cur("a1rdy")

    with ExitStack() as ses:
        sb2 = lambda n, shp, dt: ses.enter_context(nc.sbuf_tensor(n, shp, dt))
        wr_sb = sb2("a1wr", [128, 8 * D], BF16)
        wo_sb = sb2("a1wo", [128, 8 * D], BF16)
        sr = sb2("a1sr", [128, 8 * W], BF16)
        rec = sb2("a1rec", [128, 8 * W], F32)
        rwkv = sb2("a1rwkv", [128, 8 * W], BF16)
        for nm, dram, dst in [("wr", io["wrT"], wr_sb), ("wo", io["woT"], wo_sb)]:
            p.on("gpsimd", lambda e, dram=dram, dst=dst: e.dma_start(
                out=dst[:].rearrange("p (k e) -> p k e", k=8),
                in_=dram.rearrange("(k p) e -> p k e", p=128)), inc="a1wl", inc_by=16)
        t_wld2 = p.cur("a1wl")
        wrv = wr_sb[:].rearrange("p (k e) -> p k e", k=8)
        wov = wo_sb[:].rearrange("p (k e) -> p k e", k=8)
        for w in range(nwin):
            t_xl = emit_xw(w, "a1mixd")
            t_mix = emit_mix(t_xl, tmrc, omr)
            t_pev = emit_proj(t_mix, t_wld2, wrv,
                              lambda e, m, bank: e.activation(
                                  sr[:, m * W:(m + 1) * W], bank[:, 0:W], AF.Sigmoid),
                              evac_waits=[("a1rwkd", p.cur("a1rwkd"))])
            for c in range(NCH):
                p.on("sync", lambda e, w=w, c=c: e.dma_start(
                    out=vv[:, c * W:(c + 1) * W],
                    in_=io["numd"][:, c * TH + w * W:c * TH + (w + 1) * W]),
                    waits=[("a1st", stash_t[-1]), ("a1rwkd", p.cur("a1rwkd"))],
                    inc="a1nl", inc_by=16)
                p.on("sync", lambda e, w=w, c=c: e.dma_start(
                    out=ek[:, c * W:(c + 1) * W],
                    in_=io["dend"][:, c * TH + w * W:c * TH + (w + 1) * W]),
                    waits=[("a1recd", p.cur("a1recd"))], inc="a1nl", inc_by=16)
            t_nl = p.cur("a1nl")

            def fixup(v, w=w):
                v.tensor_mul(saw[:, 0:8],
                             cwtab[:].rearrange("p (c n) -> p c n", n=nwin)[:, :, w],
                             abrecv[:, 0:8])
                v.tensor_mul(saw[:, 8:16],
                             cwtab[:].rearrange("p (c n) -> p c n", n=nwin)[:, :, w],
                             abrecv[:, 8:16])
                v.drain()
                r = None
                for c in range(NCH):
                    v.scalar_tensor_tensor(vv[:, c * W:(c + 1) * W],
                                           pwtab[:, c * W:(c + 1) * W], saw[:, c:c + 1],
                                           vv[:, c * W:(c + 1) * W], ALU.mult, ALU.add)
                    r = v.scalar_tensor_tensor(ek[:, c * W:(c + 1) * W],
                                               pwtab[:, c * W:(c + 1) * W],
                                               saw[:, 8 + c:9 + c],
                                               ek[:, c * W:(c + 1) * W], ALU.mult, ALU.add)
                return r
            p.on("vector", fixup, waits=[("a1nl", t_nl), ("a1rdy", t_rdy),
                                         ("a1pwd", t_pw)], inc="a1fixd")
            t_fix = p.cur("a1fixd")

            def div_se(e):
                for c in range(NCH):
                    e.activation(rec[:, c * W:(c + 1) * W], ek[:, c * W:(c + 1) * W], AF.Ln)
                e.drain()
                r = None
                for c in range(NCH):
                    r = e.activation(rec[:, c * W:(c + 1) * W], rec[:, c * W:(c + 1) * W],
                                     AF.Exp, scale=-1.0)
                return r
            p.on("scalar", div_se, waits=[("a1fixd", t_fix),
                                          ("a1rwkd", p.cur("a1rwkd"))], inc="a1recd")
            t_rec = p.cur("a1recd")

            def rwkv_dve(v):
                for c in range(NCH):
                    v.tensor_mul(vv[:, c * W:(c + 1) * W], vv[:, c * W:(c + 1) * W],
                                 rec[:, c * W:(c + 1) * W])
                v.drain()
                r = None
                for c in range(NCH):
                    r = v.tensor_mul(rwkv[:, c * W:(c + 1) * W], vv[:, c * W:(c + 1) * W],
                                     sr[:, c * W:(c + 1) * W])
                return r
            p.on("vector", rwkv_dve, waits=[("a1recd", t_rec), ("a1pev", t_pev),
                                            ("a1womm", p.cur("a1womm"))], inc="a1rwkd")
            t_rwk = p.cur("a1rwkd")
            for j in range(W // 128):
                rj = w * W + j * 128
                ywaits = [("a1rwkd", t_rwk), ("a1wl", t_wld2), ("s3", p.cur("s3"))]

                def mm_y(e, j=j):
                    r = None
                    for q in range(2):
                        for k in range(NCH):
                            r = e.matmul(ps[6 + q][:],
                                         rwkv[:, k * W + j * 128:k * W + (j + 1) * 128],
                                         wov[:, k, q * 512:(q + 1) * 512],
                                         start=(k == 0), stop=(k == 7),
                                         skip_group_check=True)
                    return r
                p.on("tensor", mm_y, waits=ywaits, inc="a1womm")
                t_w = p.cur("a1womm")
                p.on("vector", lambda e: (e.tensor_copy(mg.s3t.ysb[:, 0:512], ps[6][:]),
                                          e.tensor_copy(mg.s3t.ysb[:, 512:1024], ps[7][:]))[-1],
                     waits=[("a1womm", t_w)], inc="yev")
                t_yev = p.cur("yev")
                emit_s3(p, mg.lns, mg.s3t, io["resin"], io["resout"], io["lnxout"],
                        rj, [("yev", t_yev)])
    oes.close()


STAGES = ["ln0", "att0", "ffn0", "att1", "ffn1"]

def build_mega(upto="ffn1", dbg_all=False, dbg_ext=(), dbg_dump=False, dbg_flags=()):
    nup = STAGES.index(upto)
    nc = bass.Bass("TRN2", target_bir_lowering=False, debug=False)
    dr0 = lambda n, shp, dt, kind: nc.dram_tensor(n, shp, dt, kind=kind).ap()
    def dr(n, shp, dt, kind):
        if kind == "Internal" and n in dbg_ext:
            kind = "ExternalOutput"
        return dr0(n, shp, dt, kind)
    EI = "ExternalInput"
    def kindof(stage):
        if dbg_all:
            return "ExternalOutput"
        return "ExternalOutput" if STAGES.index(stage) == nup else "Internal"

    arena = nc.dram_tensor("arena", [36000000], F32, kind="Internal").ap()
    _aoff = [0]

    def carve(shape, dt):
        import math as _m
        n = 1
        for s in shape:
            n *= s
        words = n if dt == F32 else (n + 1) // 2
        o = _aoff[0]
        _aoff[0] += (words + 127) // 128 * 128
        sl = arena[o:o + words]
        if dt == BF16:
            sl = sl.bitcast(BF16)
        return sl.rearrange("(a b) -> a b", b=shape[-1])

    def dri(n, shp, dt, kind):
        if kind == "Internal" and n not in dbg_ext:
            return carve(shp, dt)
        if kind == "Internal":
            kind = "ExternalOutput"
        return dr0(n, shp, dt, kind)

    xin = dr("xin", [TH, D], F32, EI)
    role = dr("role", [1, 1], U32, EI)
    prole = dr("prole", [1, 1], U32, EI)
    nonce = dr("nonce", [1, 1], U32, EI)
    flagv = dr("flagv", [128, 1], F32, EI)
    wd = {}
    for n, shp in [("Wv", [D, 2048]), ("Wg", [D, 2048]), ("Win", [D, 128]),
                   ("Wout", [2048, D]), ("bv", [2048]), ("bg", [2048]),
                   ("bout", [D]), ("vecs9", [9, 128]),
                   ("w1T", [D, 4096]), ("w2T", [D, 4096]), ("h1T", [4096, 4096]),
                   ("h2T", [4096, 4096]), ("w3T", [4096, D]), ("w1s", [4096]),
                   ("w2s", [4096]), ("h1s", [4096]), ("h2s", [4096]), ("w3s", [D]),
                   ("wkT", [D, D]), ("wvT", [D, D]), ("wrT", [D, D]), ("woT", [D, D]),
                   ("tvec5", [5, D]),
                   ("f1Win", [D, 64]), ("f1bin", [64]), ("f1W1", [64, 16]),
                   ("f1b1", [64, 16]), ("f1W2", [64, 16]), ("f1b2", [64]),
                   ("f1Wout", [64, D]), ("f1bout", [D])]:
        wd[n] = dr(n, shp, F32, EI)

    dbg_dump_list = []
    lnx0 = dri("lnx0", [TH, D], BF16, kindof("ln0"))
    if dbg_dump:
        dbg_dump_list.append(("lnx0", lnx0))
    if nup >= 1:
        vdram = dri("vdram", [TH, 2048], BF16, "Internal")
        res1 = dri("res1", [TH, D], F32, kindof("att0"))
        lnx1 = dri("lnx1", [TH, D], BF16, kindof("att0"))
        if dbg_dump:
            dbg_dump_list += [("res1", res1), ("lnx1", lnx1)]
    if nup >= 2:
        res2 = dri("res2", [TH, D], F32, kindof("ffn0"))
        lnx2p = dri("lnx2p", [TH + 129, D], BF16, kindof("ffn0"))
    if nup >= 3:
        numd = dri("numd", [128, 8 * TH], F32, "Internal")
        dend = dri("dend", [128, 8 * TH], F32, "Internal")
        res3 = dri("res3", [TH, D], F32, kindof("att1"))
        lnx3 = dri("lnx3", [TH, D], BF16, kindof("att1"))
    if nup >= 4:
        out = dr("out", [TH, D], F32, "ExternalOutput")

    sh_kv = nc.dram_tensor("sh_kv", [256, 2048], F32, addr_space="Shared")
    sh_row = nc.dram_tensor("sh_row", [2, D], BF16, addr_space="Shared")
    sh_ab = nc.dram_tensor("sh_ab", [256, 16], F32, addr_space="Shared")
    sh_flag = nc.dram_tensor("sh_flag", [2, 8], U32, addr_space="Shared")

    es = ExitStack()
    p = Prog(nc)
    mg = Mega(nc, es, p)
    sb = lambda n, shp, dt: es.enter_context(nc.sbuf_tensor(n, shp, dt))
    t_role = sb("t_role", [1, 1], U32)
    t_prole = sb("t_prole", [1, 1], U32)
    t_nonce = sb("t_nonce", [1, 1], U32)
    t_flag = sb("t_flag", [1, 1], U32)
    flagcol = sb("flagcol", [128, 1], F32)
    rowt = sb("rowt", [1, D], BF16)
    for dst, src_ in [(t_role, role), (t_prole, prole), (t_nonce, nonce),
                      (flagcol, flagv)]:
        p.on("sync", lambda e, dst=dst, src_=src_: e.dma_start(out=dst[:], in_=src_),
             inc="iold", inc_by=16)
    id_ready = mg.setup(t_role, t_prole, t_nonce, t_flag)

    m_ln0(mg, xin, lnx0)
    barrier(p)
    if nup >= 1:
        def x1(send, recv, ready):
            return emit_exchange(mg, "x1", send, sh_kv, sh_flag, 0, recv, 128, ready)
        io0 = dict(lnx0=lnx0, vdram=vdram, resin=xin, resout=res1, lnxout=lnx1,
                   Wv=wd["Wv"], Wg=wd["Wg"], Win=wd["Win"], Wout=wd["Wout"],
                   bv=wd["bv"], bg=wd["bg"], bout=wd["bout"], vecs9=wd["vecs9"],
                   flagcol=flagcol)
        m_att0(mg, io0, id_ready, x1)
        barrier(p)
    if nup >= 2:
        io1 = dict(lnx=lnx1, resin=res1, resout=res2,
                   lnxout=lnx2p[1:TH + 1, :],
                   w1T=wd["w1T"], w2T=wd["w2T"], h1T=wd["h1T"], h2T=wd["h2T"],
                   w3T=wd["w3T"], w1s=wd["w1s"], w2s=wd["w2s"], h1s=wd["h1s"],
                   h2s=wd["h2s"], w3s=wd["w3s"])
        m_ffn0(mg, io1, id_ready, dbg_flags=dbg_flags)
        barrier(p)
        xr = emit_exchange(mg, "x2", lnx2p[TH:TH + 1, :], sh_row, sh_flag, 1,
                           rowt[:], 1, [("st", p.cur("st"))])
        p.on("vector", lambda e: e.tensor_scalar(rowt[:], rowt[:],
                                                 flagcol[0:1, 0:1], None, ALU.mult),
             waits=[xr], inc="x2m")
        p.on("sync", lambda e: e.dma_start(out=lnx2p[0:1, :], in_=rowt[:]),
             waits=[("x2m", p.cur("x2m"))], inc="st", inc_by=16)
        barrier(p)
    if nup >= 3:
        def x3(send, recv, ready):
            return emit_exchange(mg, "x3", send, sh_ab, sh_flag, 2, recv, 128, ready)
        io2 = dict(lnx2p=lnx2p, resin=res2, resout=res3, lnxout=lnx3,
                   numd=numd, dend=dend, wkT=wd["wkT"], wvT=wd["wvT"],
                   wrT=wd["wrT"], woT=wd["woT"], tvec5=wd["tvec5"], flagcol=flagcol)
        m_att1(mg, io2, id_ready, x3)
        barrier(p)
    if nup >= 4:
        io3 = dict(lnx=lnx3, resin=res3, out=out, Win=wd["f1Win"], bin_=wd["f1bin"],
                   W1=wd["f1W1"], b1=wd["f1b1"], W2=wd["f1W2"], b2=wd["f1b2"],
                   Wout=wd["f1Wout"], bout=wd["f1bout"])
        m_ffn1(mg, io3, id_ready)
    if dbg_dump:
        barrier(p)
        for nm, ap_ in dbg_dump_list:
            dt = ap_.dtype
            shp = list(ap_.shape)
            od = dr0("dbg_" + nm, shp, dt, "ExternalOutput")
            p.on("sync", lambda e, od=od, ap_=ap_: e.dma_start(out=od, in_=ap_),
                 inc="st", inc_by=16)
    p.on("sync", lambda e: e.nop(), waits=[("st", p.cur("st"))])
    for name in list(p.cnt):
        p.sems[name] = es.enter_context(nc.semaphore(name))
    with nc.Block() as block:
        p.finalize_into_block(block)
    es.close()
    return nc


def mega_in_maps(inp, nonce_val):
    C = np.ascontiguousarray
    x = inp["L0_attn_x"]
    a0 = host_prep_att0(inp)
    f0 = host_prep_ffn0(inp)
    a1 = host_prep_att1(inp)
    f1 = host_prep_ffn1(inp)
    base = dict(Wv=a0["Wv"], Wg=a0["Wg"], Win=a0["Win"], Wout=a0["Wout"],
                bv=a0["bv"], bg=a0["bg"], bout=a0["bout"], vecs9=a0["vecs9"],
                w1T=f0["w1T"], w2T=f0["w2T"], h1T=f0["h1T"], h2T=f0["h2T"],
                w3T=f0["w3T"], w1s=f0["w1s"], w2s=f0["w2s"], h1s=f0["h1s"],
                h2s=f0["h2s"], w3s=f0["w3s"],
                wkT=a1["wkT"], wvT=a1["wvT"], wrT=a1["wrT"], woT=a1["woT"],
                tvec5=a1["tvec5"],
                f1Win=f1["Win"], f1bin=f1["bin_"], f1W1=f1["W1"], f1b1=f1["b1"],
                f1W2=f1["W2"], f1b2=f1["b2"], f1Wout=f1["Wout"], f1bout=f1["bout"])
    maps = []
    for c in range(8):
        b, h = c // 2, c % 2
        m = dict(base)
        m["xin"] = C(x[b, h * TH:(h + 1) * TH])
        m["role"] = np.array([[h]], dtype=np.uint32)
        m["prole"] = np.array([[1 - h]], dtype=np.uint32)
        m["nonce"] = np.array([[nonce_val]], dtype=np.uint32)
        m["flagv"] = np.full((128, 1), float(h), dtype=np.float32)
        maps.append(m)
    return maps


def kernel(**inputs):
    import time
    nonce = (int(time.time_ns()) ^ (0x9E3779B9 * 7919)) & 0xFFFFFFFF
    if nonce == 0:
        nonce = 1
    nc = build_mega("ffn1")
    maps = mega_in_maps(inputs, nonce)
    res = run_bass_kernel_spmd(nc, maps, core_ids=list(range(8)))
    B, S = 4, 8192
    out = np.empty((B, S, D), np.float32)
    for c in range(8):
        b, h = c // 2, c % 2
        out[b, h * TH:(h + 1) * TH] = res.results[c]["out"]
    return out
